# version 17
# baseline (speedup 1.0000x reference)
import contextlib
import numpy as np
import ml_dtypes

import concourse.bass as bass
import concourse.mybir as mybir
from concourse.bass_utils import run_bass_kernel_spmd

EMIT_LOG = []
F32 = mybir.dt.float32
BF16 = mybir.dt.bfloat16
I32 = mybir.dt.int32
AF = mybir.ActivationFunctionType
ALU = mybir.AluOpType
BF = ml_dtypes.bfloat16


class Cfg:
    def __init__(self, C, I, TCORE, CHUNK, SEQ, NCORES, use_collective=True):
        self.C = C
        self.I = I
        self.TCORE = TCORE
        self.CHUNK = CHUNK
        self.SEQ = SEQ
        self.NCORES = NCORES
        self.CT = C // 128
        self.IT = I // 128
        self.NCH = TCORE // CHUNK
        self.DS = -(-I // C)
        self.IPAD = self.DS * C
        self.S_REAL = 2 * self.IT + self.CT * self.DS
        self.SPC = -(-self.S_REAL // NCORES)
        self.S_PAD = self.SPC * NCORES
        self.use_collective = use_collective
        if not use_collective:
            self.SPC = self.S_PAD


CFG_FULL = Cfg(C=2048, I=5504, TCORE=2048, CHUNK=512, SEQ=4096, NCORES=8)

V_TF, V_TMK, V_TMV, V_TMR, V_XX, V_AA, V_BB, V_PP = range(8)


def build_program(cfg, lut_vals, debug=False):
    C, I, CT, IT = cfg.C, cfg.I, cfg.CT, cfg.IT
    CH, NCH = cfg.CHUNK, cfg.NCH
    DS = cfg.DS
    nc = bass.Bass("TRN2", target_bir_lowering=False, debug=False)

    xT = nc.dram_tensor("xT", [CT, 128, cfg.TCORE], F32, kind="ExternalInput").ap()
    vecs = nc.dram_tensor("vecs", [128, 8 * CT], F32, kind="ExternalInput").ap()
    gidx = nc.dram_tensor("gidx", [cfg.SPC, 128, C], I32, kind="ExternalInput").ap()
    aw = [nc.dram_tensor(f"aw{m}", [CT, 128, C], F32, kind="ExternalInput").ap()
          for m in range(4)]
    outT = nc.dram_tensor("outT", [CT, 128, cfg.TCORE], F32, kind="ExternalOutput").ap()
    dbg = {}
    if debug:
        for nm, w, dt_ in (("d_ln1", CT * CH, BF16), ("d_h", IT * CH, BF16),
                           ("d_x1", CT * CH, F32), ("d_kvr", 3 * CH, F32),
                           ("d_x2", CT * CH, F32), ("d_slab", C, BF16),
                           ("d_aw", C, BF16), ("d_rwkv", CT * CH, BF16),
                           ("d_bias", 3 * CT, F32), ("d_ckv", 3 * CT, F32),
                           ("d_vt", 8 * CT, F32), ("d_ln3", CT * CH, BF16)):
            dbg[nm] = nc.dram_tensor(nm, [128, w], dt_, kind="ExternalOutput").ap()

    GE_PAD = cfg.S_PAD * 128 * C
    if cfg.use_collective:
        cc_in = nc.dram_tensor("cc_in", [cfg.SPC * 128 * C], BF16, kind="Internal").ap()
    cc_out = nc.dram_tensor("cc_out", [GE_PAD], BF16, kind="Internal").ap()
    acache = nc.dram_tensor("acache", [4, CT, 128, C], BF16, kind="Internal").ap()

    def slab_dst_off(s):
        return s * 128 * C

    def slab_ap(s):
        off = slab_dst_off(s)
        return cc_out[off:off + 128 * C].rearrange("(p f) -> p f", p=128)

    with contextlib.ExitStack() as st:
        def sb(name, shape, dt):
            return st.enter_context(nc.sbuf_tensor(name, shape, dt))

        xs = [sb(f"xs{b}", [128, CT * CH], F32) for b in range(2)]
        lnT = sb("lnT", [128, CT * CH], BF16)
        hT = sb("hT", [128, IT * CH], BF16)
        assert IT >= CT
        rwkv = hT[:, 0:CT * CH]
        wfs = [sb(f"wfs{b}", [128, max(I, 2 * C)], BF16) for b in range(2)]
        if CT * CH * 2 >= 2 * 2 * C:
            mi = [xs[1][:, b * C:(b + 1) * C].bitcast(I32) for b in range(2)]
        else:
            mi = [sb(f"mi{b}", [128, C], I32)[:] for b in range(2)]
        was = [sb(f"was{b}", [128, C], F32) for b in range(2)]
        wab = [sb(f"wab{b}", [128, C], BF16) for b in range(2)]
        mattmp = sb("mattmp", [128, C], BF16)
        mato = [sb(f"mato{b}", [128, C], BF16) for b in range(2)]
        assert C == 4 * CH
        kv_arena = sb("kv_arena", [128, 2 * CH], F32)
        k_s = kv_arena[:, 0:CH]
        v_s = kv_arena[:, CH:2 * CH]
        matbf = kv_arena[:].bitcast(BF16)
        r_s = sb("r_s", [128, CH], F32)
        tmpA = sb("tmpA", [128, CH], F32)
        tmpC = sb("tmpC", [128, CH], F32)
        sq = [sb(f"sq{b}", [128, CH], F32) for b in range(2)]
        tmpB = sq[1]
        tmpD = sq[0]
        sg = sb("sg", [128, CH], BF16)
        vt = sb("vt", [128, 8 * CT], F32)
        ckv = sb("ckv", [128, 3 * CT], F32)
        bias = sb("bias", [128, 3 * CT], F32)
        rows = sb("rows", [1, 5 * CH], F32)
        fx = sb("fx", [128, max(8, CT)], F32)
        ones1 = sb("ones1", [1, 128], F32)
        onesk = sb("onesk", [128, 1], F32)
        onesb = sb("onesb", [128, 1], BF16)
        sqb = [sq[b][:].bitcast(BF16) for b in range(2)]

        PS = [st.enter_context(nc.psum_tensor(f"ps{b}", [128, 512], F32))
              for b in range(8)]
        PS_G, PS_U, PS_D0, PS_D1, PS_S1, PS_S2, PS_BA, PS_BB = PS

        class Sem:
            def __init__(self, name):
                self.h = st.enter_context(nc.semaphore(name))
                self.n = 0
                self.name = name

        sPE, sDV, sAC = Sem("sPE"), Sem("sDV"), Sem("sAC")
        sLX, sSX = Sem("sLX"), Sem("sSX")
        sWF = [Sem("sWF0"), Sem("sWF1")]
        sWA = [Sem("sWA0"), Sem("sWA1")]
        sWB = [Sem("sWB0"), Sem("sWB1")]
        sMI = [Sem("sMI0"), Sem("sMI1")]
        sSW = Sem("sSW")
        sAWR = Sem("sAWR")
        sCC = Sem("sCC")
        sLV = Sem("sLV")
        sONE = Sem("sONE")

        last_wait = {}

        def W(eng, ekey, sem, val):
            if val <= 0:
                return
            assert val <= sem.n, (sem.name, val, sem.n)
            if last_wait.get((ekey, sem.name), 0) >= val:
                return
            eng.wait_ge(sem.h, val)
            last_wait[(ekey, sem.name)] = val

        def inc(instr, sem, amt=1):
            instr.then_inc(sem.h, amt)
            sem.n += amt
            return sem.n

        block = st.enter_context(nc.Block())

        bank_free = {id(p): (sDV, 0) for p in PS}
        bank_ready = {}

        slot_free_wf = [(sPE, 0), (sPE, 0)]
        slot_free_wa = [(sPE, 0), (sPE, 0)]
        x_ready = [[(sDV, 0)] * CT for _ in range(NCH)]
        ln_free = (sPE, 0)
        kvr_free = (sDV, 0)
        sg_free = (sDV, 0)
        rwkv_cnt = 0

        @block.gpsimd
        def _(g):
            inc(g.dma_start(out=vt[:], in_=vecs), sLV, 16)
            for ct in range(CT):
                inc(g.dma_start(out=xs[0][:, ct * CH:(ct + 1) * CH],
                                in_=xT[ct, :, 0:CH]), sLX, 16)

        @block.vector
        def _(v):
            W(v, "V", sLV, 16)
            for m, vrow in enumerate((V_TMK, V_TMV, V_TMR)):
                tm = vt[:, vrow * CT:(vrow + 1) * CT]
                xxv = vt[:, V_XX * CT:(V_XX + 1) * CT]
                dst = ckv[:, m * CT:(m + 1) * CT]
                v.tensor_tensor(fx[:, 0:CT], xxv, tm, ALU.mult)
                v.drain()
                v.tensor_tensor(dst, xxv, fx[:, 0:CT], ALU.subtract)
                v.drain()
            inc(v.drain(), sDV, 1)
        ck_ms = sDV.n

        @block.gpsimd
        def _(g):
            g.memset(ones1[:], 1.0)
            g.memset(onesb[:], 1.0)
            inc(g.memset(onesk[:], 1.0), sONE, 1)
        ones_ms = sONE.n

        mat_dve_done = [0] * max(cfg.SPC, 1)

        @block.gpsimd
        def _(g):
            for k0 in range(min(2, cfg.SPC)):
                inc(g.dma_start(out=mi[k0][:, 0:C], in_=gidx[k0]), sMI[k0], 16)

        for k in range(cfg.SPC):
            slot = k % 2

            @block.vector
            def _(v, k=k, slot=slot):
                W(v, "V", sMI[slot], 16 * (k // 2 + 1))
                if k >= 2:
                    W(v, "V", sSW, 16 * (k - 1))
                v.tensor_copy(matbf[:], mi[slot][:, 0:C])
                mo = mato[k % 2]
                v.tensor_scalar(mo[:], matbf[:], 0.0, float(lut_vals[0]),
                                ALU.is_equal, ALU.mult)
                for j in range(1, 16):
                    v.tensor_scalar(mattmp[:], matbf[:], float(j),
                                    float(lut_vals[j]), ALU.is_equal, ALU.mult)
                    lastv = v.tensor_tensor(mo[:], mo[:], mattmp[:], ALU.add)
                inc(lastv, sDV, 1)
                mat_dve_done[k] = sDV.n

            @block.gpsimd
            def _(g, k=k, slot=slot):
                W(g, "G", sDV, mat_dve_done[k])
                if k + 2 < cfg.SPC:
                    inc(g.dma_start(out=mi[slot][:, 0:C], in_=gidx[k + 2]),
                        sMI[slot], 16)
                if cfg.use_collective:
                    dst = cc_in[k * 128 * C:(k + 1) * 128 * C].rearrange(
                        "(p f) -> p f", p=128)
                else:
                    dst = cc_out[k * 128 * C:(k + 1) * 128 * C].rearrange(
                        "(p f) -> p f", p=128)
                inc(g.dma_start(out=dst, in_=mato[k % 2][:]), sSW, 16)
                if cfg.use_collective:
                    W(g, "G", sSW, 16 * (k + 1))
                    gsz = cfg.NCORES * 128 * C
                    inc(g.collective_compute(
                        "AllGather", ALU.bypass,
                        replica_groups=[list(range(cfg.NCORES))],
                        ins=[cc_in[k * 128 * C:(k + 1) * 128 * C]
                             .rearrange("(r c) -> r c", c=C).opt()],
                        outs=[cc_out[k * gsz:(k + 1) * gsz]
                              .rearrange("(r c) -> r c", c=C).opt()],
                    ), sCC, 1)
                else:
                    inc(g.nop(), sCC, 1)

        @block.gpsimd
        def _(g):
            W(g, "G", sSW, sSW.n)

        bias_ms = [0]

        def conv_slab(j):
            m, ot = j // CT, j % CT
            slot = j % 2

            @block.sync
            def _(y):
                fs, fv = slot_free_wa[slot]
                W(y, "Y", fs, fv)
                inc(y.dma_start(out=was[slot][:], in_=aw[m][ot]), sWA[slot], 16)

            if m < 3:
                @block.tensor
                def _(t):
                    W(t, "T", sWA[slot], sWA[slot].n)
                    W(t, "T", sDV, ck_ms)
                    for ct in range(CT):
                        mm = t.matmul(PS_D1[:, j:j + 1],
                                      was[slot][:, ct * 128:(ct + 1) * 128],
                                      ckv[:, m * CT + ct:m * CT + ct + 1],
                                      start=(ct == 0), stop=(ct == CT - 1))
                    inc(mm, sPE, 1)

            @block.vector
            def _(v):
                W(v, "V", sLV, 16)
                if m < 3:
                    W(v, "V", sPE, sPE.n)
                else:
                    W(v, "V", sWA[slot], sWA[slot].n)
                W(v, "V", sAWR, 16 * j)
                if m < 3:
                    for ct in range(CT):
                        v.tensor_scalar(wab[1][:, ct * 128:(ct + 1) * 128],
                                        was[slot][:, ct * 128:(ct + 1) * 128],
                                        vt[:, (V_TMK + m) * CT + ct:
                                           (V_TMK + m) * CT + ct + 1],
                                        None, ALU.mult)
                else:
                    v.tensor_copy(wab[1][:], was[slot][:])
                if j == 3 * CT - 1:
                    v.tensor_tensor(bias[:, 0:CT], PS_D1[:, 0:CT],
                                    vt[:, V_TF * CT:(V_TF + 1) * CT], ALU.add)
                    v.tensor_copy(bias[:, CT:3 * CT], PS_D1[:, CT:3 * CT])
                inc(v.drain(), sDV, 1)
                slot_free_wa[slot] = (sDV, sDV.n)
                if j == 3 * CT - 1:
                    bank_free[id(PS_D1)] = (sDV, sDV.n)
            if j == 3 * CT - 1:
                bias_ms[0] = sDV.n
            cms = sDV.n

            @block.sync
            def _(y):
                W(y, "Y", sDV, cms)
                inc(y.dma_start(out=acache[m, ot], in_=wab[1][:]), sAWR, 16)

        def conv_finish():
            @block.sync
            def _(y):
                W(y, "Y", sAWR, sAWR.n)


        def make_stats_hook(ch, xset, same_phase, bf16_ok=True):
            st_state = {"pe": [0, 0], "ms": 0}

            def hook(ct, xr=None):
                b = ct % 2
                xrs, xrv = xr if (same_phase and xr is not None) \
                    else x_ready[ch][ct]

                @block.scalar
                def _(a, ct=ct, b=b, xrs=xrs, xrv=xrv):
                    W(a, "A", xrs, xrv)
                    if ct == 0:
                        W(a, "A", sLX, 16 * CT * (ch + 1))
                    if ct >= 2:
                        W(a, "A", sPE, st_state["pe"][b])
                    if bf16_ok:
                        a.activation(lnT[:, ct * CH:(ct + 1) * CH],
                                     xs[xset][:, ct * CH:(ct + 1) * CH],
                                     AF.Identity)
                        inc(a.activation(sqb[b][:, 0:CH],
                                         xs[xset][:, ct * CH:(ct + 1) * CH],
                                         AF.Square), sAC, 1)
                    else:
                        inc(a.activation(sq[b][:, 0:CH],
                                         xs[xset][:, ct * CH:(ct + 1) * CH],
                                         AF.Square), sAC, 1)

                @block.tensor
                def _(t, ct=ct, b=b, xrs=xrs, xrv=xrv):
                    W(t, "T", sAC, sAC.n)
                    W(t, "T", xrs, xrv)
                    if ct == 0:
                        W(t, "T", sLX, 16 * CT * (ch + 1))
                        W(t, "T", sONE, ones_ms)
                        for p in (PS_S1, PS_S2):
                            bs, bv = bank_free[id(p)]
                            W(t, "T", bs, bv)
                    if bf16_ok:
                        t.matmul(PS_S1[0:1, 0:CH], onesb[:],
                                 lnT[:, ct * CH:(ct + 1) * CH],
                                 start=(ct == 0), stop=(ct == CT - 1))
                        mm = t.matmul(PS_S2[0:1, 0:CH], onesb[:],
                                      sqb[b][:, 0:CH],
                                      start=(ct == 0), stop=(ct == CT - 1))
                    else:
                        t.matmul(PS_S1[0:1, 0:CH], onesk[:],
                                 xs[xset][:, ct * CH:(ct + 1) * CH],
                                 start=(ct == 0), stop=(ct == CT - 1))
                        mm = t.matmul(PS_S2[0:1, 0:CH], onesk[:], sq[b][:, 0:CH],
                                      start=(ct == 0), stop=(ct == CT - 1))
                    inc(mm, sPE, 1)
                    st_state["pe"][b] = sPE.n
                    if ct == CT - 1:
                        st_state["ms"] = sPE.n
            return hook, st_state

        def emit_ln(ch, xset, tag, stats_done=None):
            nonlocal ln_free
            sl = slice(0, CH)

            if stats_done is None:
                hook, st_state = make_stats_hook(ch, xset, same_phase=False,
                                                 bf16_ok=False)
                for ct in range(CT):
                    hook(ct)
                stats_ms = st_state["ms"]
            else:
                stats_ms = stats_done
            bank_ready[id(PS_S1)] = (sPE, stats_ms)
            bank_ready[id(PS_S2)] = (sPE, stats_ms)

            m_row = rows[0:1, 0:CH]
            v_row = rows[0:1, CH:2 * CH]
            a_row = rows[0:1, 2 * CH:3 * CH]
            b_row = rows[0:1, 3 * CH:4 * CH]
            t_row = rows[0:1, 4 * CH:5 * CH]
            t2_row = v_row

            @block.vector
            def _(v):
                W(v, "V", sPE, stats_ms)
                v.tensor_scalar(m_row, PS_S1[0:1, sl], 1.0 / C, None, ALU.mult)
                v.tensor_scalar(v_row, PS_S2[0:1, sl], 1.0 / C, 1e-5,
                                ALU.mult, ALU.add)
                v.drain()
                v.tensor_tensor(t_row, m_row, m_row, ALU.mult)
                v.drain()
                v.tensor_tensor(v_row, v_row, t_row, ALU.subtract)
                inc(v.drain(), sDV, 1)
                bank_free[id(PS_S1)] = (sDV, sDV.n)
                bank_free[id(PS_S2)] = (sDV, sDV.n)
                EMIT_LOG.append(("rows_read", ch, tag, stats_ms, sDV.n))

            @block.scalar
            def _(a):
                W(a, "A", sDV, sDV.n)
                a.activation(t_row, v_row, AF.Ln)
                inc(a.drain(), sAC, 1)

            @block.vector
            def _(v):
                W(v, "V", sAC, sAC.n)
                v.tensor_scalar(t2_row, t_row, -0.5, None, ALU.mult)
                inc(v.drain(), sDV, 1)

            @block.scalar
            def _(a):
                W(a, "A", sDV, sDV.n)
                a.activation(a_row, t2_row, AF.Exp)
                inc(a.drain(), sAC, 1)

            @block.vector
            def _(v):
                W(v, "V", sAC, sAC.n)
                v.tensor_tensor(t_row, m_row, a_row, ALU.mult)
                v.drain()
                v.tensor_scalar(b_row, t_row, -1.0, None, ALU.mult)
                inc(v.drain(), sDV, 1)
            rows_ms = sDV.n

            @block.tensor
            def _(t):
                W(t, "T", sDV, rows_ms)
                for p in (PS_BA, PS_BB):
                    bs, bv = bank_free[id(p)]
                    W(t, "T", bs, bv)
                t.matmul(PS_BA[:, sl], ones1[:], a_row, start=True, stop=True)
                mm = t.matmul(PS_BB[:, sl], ones1[:], b_row, start=True, stop=True)
                inc(mm, sPE, 1)
                bank_ready[id(PS_BA)] = (sPE, sPE.n)
                bank_ready[id(PS_BB)] = (sPE, sPE.n)
            bc_ms = sPE.n

            ln_tiles = [0] * CT

            @block.vector
            def _(v):
                W(v, "V", sPE, bc_ms)
                fs, fv = ln_free
                W(v, "V", fs, fv)
                for ct in range(CT):
                    xap = xs[xset][:, ct * CH:(ct + 1) * CH]
                    v.tensor_tensor(tmpA[:, sl], xap, PS_BA[:, sl], ALU.mult)
                    inc(v.tensor_tensor(lnT[:, ct * CH:(ct + 1) * CH],
                                        tmpA[:, sl], PS_BB[:, sl], ALU.add),
                        sDV, 1)
                    ln_tiles[ct] = sDV.n
                bank_free[id(PS_BA)] = (sDV, sDV.n)
                bank_free[id(PS_BB)] = (sDV, sDV.n)
            return ln_tiles

        fs_cnt = [0]

        def ffn_slab_load(src_slabs, slot):
            if cfg.use_collective:
                need = max(s // cfg.NCORES for s, _ in src_slabs) + 1
            else:
                need = max(s for s, _ in src_slabs) + 1

            @block.sync
            def _(y):
                W(y, "Y", sCC, need)
                fsm, fvv = slot_free_wf[slot]
                W(y, "Y", fsm, fvv)
                for s, coff in src_slabs:
                    w = min(C, I - coff)
                    inc(y.dma_start(out=wfs[slot][:, coff:coff + w],
                                    in_=slab_ap(s)[:, 0:w]), sWF[slot], 16)

        conv_queue = list(range(4 * CT))
        assert 3 * CT <= 2 * IT

        def emit_ffn(ch, xset, ln_ms, conv_hook=None, tail_hook=None):
            nonlocal sg_free
            sl = slice(0, CH)
            h_ms = 0
            silu_ms = {}
            for it in range(IT):
                if conv_hook is not None and conv_queue:
                    for _ in range(2):
                        if conv_queue:
                            conv_hook(conv_queue.pop(0))
                for half, base in ((0, 0), (1, IT)):
                    slot = fs_cnt[0] % 2
                    fs_cnt[0] += 1
                    ffn_slab_load([(base + it, 0)], slot)

                    @block.tensor
                    def _(t, it=it, half=half, slot=slot):
                        W(t, "T", sWF[slot], sWF[slot].n)
                        p = PS_G if half == 0 else PS_U
                        bs, bv = bank_free[id(p)]
                        W(t, "T", bs, bv)
                        for ht in range(CT):
                            W(t, "T", sDV, ln_ms[ht])
                            mm = t.matmul(p[:, sl],
                                          wfs[slot][:, ht * 128:(ht + 1) * 128],
                                          lnT[:, ht * CH:(ht + 1) * CH],
                                          start=(ht == 0), stop=(ht == CT - 1))
                        inc(mm, sPE, 1)
                        bank_ready[id(p)] = (sPE, sPE.n)
                        slot_free_wf[slot] = (sPE, sPE.n)

                @block.scalar
                def _(a, it=it):
                    rs, rv = bank_ready[id(PS_G)]
                    W(a, "A", rs, rv)
                    fs2, fv2 = sg_free
                    W(a, "A", fs2, fv2)
                    inc(a.activation(sg[:, sl], PS_G[:, sl], AF.Silu), sAC, 1)
                    silu_ms[it] = sAC.n
                    bank_free[id(PS_G)] = (sAC, sAC.n)

                @block.vector
                def _(v, it=it):
                    W(v, "V", sAC, silu_ms[it])
                    rs, rv = bank_ready[id(PS_U)]
                    W(v, "V", rs, rv)
                    inc(v.tensor_tensor(hT[:, it * CH:(it + 1) * CH], sg[:, sl],
                                        PS_U[:, sl], ALU.mult), sDV, 1)
                    bank_free[id(PS_U)] = (sDV, sDV.n)
                    sg_free = (sDV, sDV.n)
                h_ms = sDV.n

            res_list = [None] * CT
            for ot in range(CT):
                if ot >= 2 and tail_hook is not None:
                    tail_hook(ot - 2, res_list[ot - 2])
                slot = fs_cnt[0] % 2
                fs_cnt[0] += 1
                ffn_slab_load([(2 * IT + ot * DS + j, j * C) for j in range(DS)],
                              slot)
                p = PS_D0 if ot % 2 == 0 else PS_D1

                @block.tensor
                def _(t, ot=ot, slot=slot, p=p):
                    W(t, "T", sWF[slot], sWF[slot].n)
                    W(t, "T", sDV, h_ms)
                    bs, bv = bank_free[id(p)]
                    W(t, "T", bs, bv)
                    for it in range(IT):
                        mm = t.matmul(p[:, sl],
                                      wfs[slot][:, it * 128:(it + 1) * 128],
                                      hT[:, it * CH:(it + 1) * CH],
                                      start=(it == 0), stop=(it == IT - 1))
                    inc(mm, sPE, 1)
                    bank_ready[id(p)] = (sPE, sPE.n)
                    slot_free_wf[slot] = (sPE, sPE.n)

                @block.vector
                def _(v, ot=ot, p=p):
                    rs, rv = bank_ready[id(p)]
                    W(v, "V", rs, rv)
                    inc(v.scalar_tensor_tensor(
                        xs[xset][:, ot * CH:(ot + 1) * CH], p[:, sl], 0.5,
                        xs[xset][:, ot * CH:(ot + 1) * CH], ALU.mult, ALU.add),
                        sDV, 1)
                    bank_free[id(p)] = (sDV, sDV.n)
                res_list[ot] = (sDV, sDV.n)
            if tail_hook is not None:
                for ct in range(CT - 2, CT):
                    tail_hook(ct, res_list[ct])
            return res_list

        a_cnt = [4 * CT]

        def attn_slab_load(m, ot, slot):
            @block.sync
            def _(y):
                fsm, fvv = slot_free_wa[slot]
                W(y, "Y", fsm, fvv)
                inc(y.dma_start(out=wab[slot][:], in_=acache[m, ot]), sWA[slot], 16)

        def emit_attn(ch, xset, ln_ms, tail_hook=None):
            nonlocal kvr_free, rwkv_cnt, ln_free
            sl = slice(0, CH)
            chain_ms = 0
            for ot in range(CT):
                psum_of = {0: PS_G, 1: PS_U, 2: PS_D0}
                for m in range(3):
                    slot = a_cnt[0] % 2
                    a_cnt[0] += 1
                    attn_slab_load(m, ot, slot)
                    p = psum_of[m]

                    @block.tensor
                    def _(t, m=m, ot=ot, slot=slot, p=p):
                        W(t, "T", sWA[slot], sWA[slot].n)
                        bs, bv = bank_free[id(p)]
                        W(t, "T", bs, bv)
                        for ct in range(CT):
                            W(t, "T", sDV, ln_ms[ct])
                            mm = t.matmul(p[:, sl],
                                          wab[slot][:, ct * 128:(ct + 1) * 128],
                                          lnT[:, ct * CH:(ct + 1) * CH],
                                          start=(ct == 0), stop=(ct == CT - 1))
                        inc(mm, sPE, 1)
                        bank_ready[id(p)] = (sPE, sPE.n)
                        slot_free_wa[slot] = (sPE, sPE.n)

                @block.scalar
                def _(a, ot=ot):
                    W(a, "A", sDV, bias_ms[0])
                    fs2, fv2 = kvr_free
                    W(a, "A", fs2, fv2)
                    for m, (p, dst) in enumerate(((PS_G, k_s), (PS_U, v_s),
                                                  (PS_D0, r_s))):
                        rs, rv = bank_ready[id(p)]
                        W(a, "A", rs, rv)
                        fn = AF.Sigmoid if m == 2 else AF.Identity
                        lasta = a.activation(
                            dst[:, sl], p[:, sl], fn,
                            bias=bias[:, m * CT + ot:m * CT + ot + 1],
                            scale=1.0)
                    inc(lasta, sAC, 1)
                    bank_free[id(PS_G)] = (sAC, sAC.n)
                    bank_free[id(PS_U)] = (sAC, sAC.n)
                    bank_free[id(PS_D0)] = (sAC, sAC.n)
                evac_ms = sAC.n

                @block.vector
                def _(v, ot=ot):
                    W(v, "V", sAC, evac_ms)
                    v.tensor_scalar(tmpA[:, sl], k_s[:, sl], 0.0, None, ALU.min)
                    lastv = v.tensor_scalar(tmpC[:, sl], k_s[:, sl], -1.0, 0.0,
                                            ALU.mult, ALU.max)
                    inc(lastv, sDV, 1)
                m2_ms = sDV.n

                @block.scalar
                def _(a, ot=ot):
                    W(a, "A", sDV, m2_ms)
                    a.activation(tmpB[:, sl], tmpA[:, sl], AF.Exp)
                    lasta = a.activation(tmpD[:, sl], tmpC[:, sl], AF.Exp)
                    inc(lasta, sAC, 1)
                exp_ms = sAC.n
                if debug and ch == 0 and ot == 0:
                    @block.gpsimd
                    def _(g):
                        W(g, "G", sAC, evac_ms)
                        inc(g.dma_start(out=dbg["d_kvr"][:, 0:CH],
                                        in_=k_s[:, sl]), sSX, 16)
                        inc(g.dma_start(out=dbg["d_kvr"][:, CH:2 * CH],
                                        in_=v_s[:, sl]), sSX, 16)
                        inc(g.dma_start(out=dbg["d_kvr"][:, 2 * CH:3 * CH],
                                        in_=r_s[:, sl]), sSX, 16)

                @block.vector
                def _(v, ot=ot):
                    W(v, "V", sAC, exp_ms)
                    v.tensor_tensor(tmpB[:, sl], tmpB[:, sl], v_s[:, sl], ALU.mult)
                    v.tensor_tensor(tmpB[:, sl], tmpB[:, sl], tmpD[:, sl], ALU.mult)
                    inc(v.tensor_tensor(rwkv[:, ot * CH:(ot + 1) * CH],
                                        tmpB[:, sl], r_s[:, sl], ALU.mult),
                        sDV, 1)

                if ch == 0:
                    f1 = fx[:, 0:1]; f2 = fx[:, 1:2]; f3 = fx[:, 2:3]
                    f4 = fx[:, 3:4]; f5 = fx[:, 4:5]; f6 = fx[:, 5:6]
                    f7 = fx[:, 6:7]; f8 = fx[:, 7:8]
                    ppc = vt[:, V_PP * CT + ot:V_PP * CT + ot + 1]
                    aac = vt[:, V_AA * CT + ot:V_AA * CT + ot + 1]
                    bbc = vt[:, V_BB * CT + ot:V_BB * CT + ot + 1]

                    @block.vector
                    def _(v, ot=ot):
                        v.tensor_tensor(f1, ppc, k_s[:, 0:1], ALU.max)
                        v.drain()
                        v.tensor_tensor(f2, ppc, f1, ALU.subtract)
                        v.tensor_tensor(f3, k_s[:, 0:1], f1, ALU.subtract)
                        inc(v.drain(), sDV, 1)

                    @block.scalar
                    def _(a, ot=ot):
                        W(a, "A", sDV, sDV.n)
                        a.activation(f6, f2, AF.Exp)
                        a.activation(f7, f3, AF.Exp)
                        inc(a.drain(), sAC, 1)

                    @block.vector
                    def _(v, ot=ot):
                        W(v, "V", sAC, sAC.n)
                        v.tensor_tensor(f4, f6, aac, ALU.mult)
                        v.tensor_tensor(f5, f7, v_s[:, 0:1], ALU.mult)
                        v.drain()
                        v.tensor_tensor(f4, f4, f5, ALU.add)
                        v.tensor_tensor(f2, f6, bbc, ALU.mult)
                        v.drain()
                        v.tensor_tensor(f2, f2, f7, ALU.add)
                        v.drain()
                        v.reciprocal(f8, f2)
                        v.drain()
                        v.tensor_tensor(f4, f4, f8, ALU.mult)
                        v.drain()
                        v.tensor_tensor(rwkv[:, ot * CH:ot * CH + 1], f4,
                                        r_s[:, 0:1], ALU.mult)
                        inc(v.drain(), sDV, 1)

                chain_ms = sDV.n
                kvr_free = (sDV, chain_ms)

            res_list = [None] * CT
            for ot in range(CT):
                if ot >= 2 and tail_hook is not None:
                    tail_hook(ot - 2, res_list[ot - 2])
                slot = a_cnt[0] % 2
                a_cnt[0] += 1
                attn_slab_load(3, ot, slot)
                p = PS_D1 if ot % 2 == 0 else PS_D0

                @block.tensor
                def _(t, ot=ot, slot=slot, p=p):
                    W(t, "T", sWA[slot], sWA[slot].n)
                    W(t, "T", sDV, chain_ms)
                    bs, bv = bank_free[id(p)]
                    W(t, "T", bs, bv)
                    for ct in range(CT):
                        mm = t.matmul(p[:, sl],
                                      wab[slot][:, ct * 128:(ct + 1) * 128],
                                      rwkv[:, ct * CH:(ct + 1) * CH],
                                      start=(ct == 0), stop=(ct == CT - 1))
                    inc(mm, sPE, 1)
                    bank_ready[id(p)] = (sPE, sPE.n)
                    slot_free_wa[slot] = (sPE, sPE.n)
                if ot == CT - 1:
                    ln_free = (sPE, sPE.n)

                @block.vector
                def _(v, ot=ot, p=p):
                    rs, rv = bank_ready[id(p)]
                    W(v, "V", rs, rv)
                    inc(v.scalar_tensor_tensor(
                        xs[xset][:, ot * CH:(ot + 1) * CH], p[:, sl], 1.0,
                        xs[xset][:, ot * CH:(ot + 1) * CH], ALU.mult, ALU.add),
                        sDV, 1)
                    bank_free[id(p)] = (sDV, sDV.n)
                res_list[ot] = (sDV, sDV.n)
            if tail_hook is not None:
                for ct in range(CT - 2, CT):
                    tail_hook(ct, res_list[ct])
            return res_list

        def dump(name, src_ap, width, wait_sem, wait_val):
            if not debug:
                return
            @block.gpsimd
            def _(g):
                W(g, "G", wait_sem, wait_val)
                inc(g.dma_start(out=dbg[name], in_=src_ap), sSX, 16)

        def dump_dram(name, src_ap, width, wait_sem, wait_val):
            if not debug:
                return
            @block.gpsimd
            def _(g):
                W(g, "G", wait_sem, wait_val)
                inc(g.dma_start(out=dbg[name], in_=src_ap), sSX, 16)

        if debug:
            dump_dram("d_slab", slab_ap(0), C, sCC, 1)
            dump("d_ckv", ckv[:], 3 * CT, sDV, ck_ms)
            dump("d_vt", vt[:], 8 * CT, sLV, 16)

        store_ms = [0] * (NCH + 2)

        def emit_xload(ch):
            if ch == 0 or ch >= NCH:
                return

            @block.gpsimd
            def _(g, ch=ch):
                if ch >= 2:
                    W(g, "G", sSX, store_ms[ch - 2])
                for ct in range(CT):
                    inc(g.dma_start(
                        out=xs[ch % 2][:, ct * CH:(ct + 1) * CH],
                        in_=xT[ct, :, ch * CH:(ch + 1) * CH]), sLX, 16)

        def emit_ffn1(ch, stats_done=None, tail_hook=None):
            xset = ch % 2
            ln1 = emit_ln(ch, xset, "ffn1", stats_done=stats_done)
            if ch == 0:
                dump("d_ln1", lnT[:], CT * CH, sDV, ln1[-1])
            r1 = emit_ffn(ch, xset, ln1,
                          conv_hook=(conv_slab if ch <= 1 else None),
                          tail_hook=tail_hook)
            x_ready[ch] = r1
            if ch == 0:
                dump("d_h", hT[:], IT * CH, sDV, r1[-1][1])
                dump("d_x1", xs[xset][:], CT * CH, sDV, r1[-1][1])

        def emit_rest(ch, pre_attn_stats=None, next_hook=None, use_hook2=True):
            xset = ch % 2
            lna = emit_ln(ch, xset, "attn", stats_done=pre_attn_stats)
            if use_hook2:
                hook2, st2 = make_stats_hook(ch, xset, same_phase=True)
            r2 = emit_attn(ch, xset, lna,
                           tail_hook=(hook2 if use_hook2 else None))
            x_ready[ch] = r2
            if ch == 0:
                dump("d_x2", xs[xset][:], CT * CH, sDV, r2[-1][1])
            ln2 = emit_ln(ch, xset, "ffn2",
                          stats_done=(st2["ms"] if use_hook2 else None))
            if ch == 0:
                dump("d_ln3", lnT[:], CT * CH, sDV, ln2[-1])
            r3 = emit_ffn(ch, xset, ln2, tail_hook=next_hook)
            x_ready[ch] = r3

            @block.gpsimd
            def _(g, ch=ch, xset=xset, r3=r3):
                W(g, "G", sDV, r3[-1][1])
                for ct in range(CT):
                    inc(g.dma_start(out=outT[ct, :, ch * CH:(ch + 1) * CH],
                                    in_=xs[xset][:, ct * CH:(ct + 1) * CH]),
                        sSX, 16)
            store_ms[ch] = sSX.n

        emit_xload(0)
        emit_xload(1)
        HOOKS = True
        HOOK2 = True
        emit_ffn1(0)
        hA0, stA0 = make_stats_hook(0, 0, same_phase=False)
        emit_ffn1(1, tail_hook=(hA0 if HOOKS else None))
        conv_finish()
        if debug:
            dump_dram("d_aw", acache[0, 0], C, sAWR, 16 * 1)
            dump("d_bias", bias[:], 3 * CT, sDV, bias_ms[0])
        hooks = {}
        if NCH > 1:
            hooks[1], hooks["s1"] = make_stats_hook(1, 1, same_phase=False)
        if NCH > 2:
            hooks[2], hooks["s2"] = make_stats_hook(2, 0, same_phase=False)
        if NCH > 3:
            hooks[3], hooks["s3"] = make_stats_hook(3, 1, same_phase=False)
        emit_rest(0, use_hook2=HOOK2, pre_attn_stats=(stA0["ms"] if HOOKS else None),
                  next_hook=(hooks.get(1) if HOOKS else None))
        emit_xload(2)
        emit_rest(1, use_hook2=HOOK2,
                  pre_attn_stats=(hooks["s1"]["ms"] if HOOKS and NCH > 1 else None),
                  next_hook=(hooks.get(2) if HOOKS else None))
        emit_xload(3)
        if NCH > 2:
            emit_ffn1(2, stats_done=(hooks["s2"]["ms"] if HOOKS else None))
            emit_rest(2, use_hook2=HOOK2, pre_attn_stats=None,
                      next_hook=(hooks.get(3) if HOOKS else None))
        if NCH > 3:
            emit_ffn1(3, stats_done=(hooks["s3"]["ms"] if HOOKS else None))
            emit_rest(3, use_hook2=HOOK2, pre_attn_stats=None, next_hook=None)

        @block.gpsimd
        def _(g):
            W(g, "G", sSX, store_ms[NCH - 1])

    return nc



def _wstream(w, CT):
    wT = np.ascontiguousarray(np.asarray(w, dtype=np.float32).T)
    C = wT.shape[0]
    s = wT.reshape(CT, 128, CT, 128).transpose(2, 1, 0, 3)
    return np.ascontiguousarray(s).reshape(CT, 128, C)


def _ghost_stream(gate_idx, up_idx, down_idx, cfg):
    C, I, CT, IT, DS = cfg.C, cfg.I, cfg.CT, cfg.IT, cfg.DS
    gT = np.ascontiguousarray(np.asarray(gate_idx, dtype=np.int32).T)
    uT = np.ascontiguousarray(np.asarray(up_idx, dtype=np.int32).T)
    dT = np.ascontiguousarray(np.asarray(down_idx, dtype=np.int32).T)
    g = gT.reshape(CT, 128, IT, 128).transpose(2, 1, 0, 3).reshape(IT, 128, C)
    u = uT.reshape(CT, 128, IT, 128).transpose(2, 1, 0, 3).reshape(IT, 128, C)
    d = dT.reshape(IT, 128, CT, 128).transpose(2, 1, 0, 3).reshape(CT, 128, I)
    dpad = np.zeros((CT, 128, cfg.IPAD), dtype=np.int32)
    dpad[:, :, :I] = d
    d_sl = dpad.reshape(CT, 128, DS, C).transpose(0, 2, 1, 3)
    flat = np.concatenate([g.reshape(-1), u.reshape(-1), d_sl.reshape(-1)])
    pad = np.zeros(cfg.S_PAD * 128 * C, dtype=np.int32)
    pad[:flat.size] = flat
    return pad.reshape(cfg.S_PAD, 128, C)


def _core_slabs(ghost, cfg, r):
    if not cfg.use_collective:
        return ghost
    ks = [cfg.NCORES * k + r for k in range(cfg.SPC)]
    return np.ascontiguousarray(ghost[ks])


def _vecs_for_core(tf, tmk, tmv, tmr, xx, aa, bb, pp, cfg, seq_start):
    C, CT = cfg.C, cfg.CT
    z = np.zeros(C, dtype=np.float32)
    neg = np.full(C, -1e38, dtype=np.float32)
    rows = [np.asarray(v, dtype=np.float32).reshape(C) for v in
            (tf, tmk, tmv, tmr, xx)]
    rows += [np.asarray(aa, np.float32).reshape(C) if seq_start else z,
             np.asarray(bb, np.float32).reshape(C) if seq_start else z,
             np.asarray(pp, np.float32).reshape(C) if seq_start else neg]
    v2 = np.stack(rows)
    return np.ascontiguousarray(
        v2.reshape(8, CT, 128).transpose(2, 0, 1).reshape(128, 8 * CT))


def run(cfg, inputs):
    x = np.asarray(inputs["L0_attn_x"], dtype=np.float32)
    B, T, C = x.shape
    xf = x.reshape(B * T, C)
    lut = [float(v) for v in np.asarray(inputs["L0_ffn_lut"], dtype=np.float32)]

    ghost = _ghost_stream(inputs["L0_ffn_gate_idx"], inputs["L0_ffn_up_idx"],
                          inputs["L0_ffn_down_idx"], cfg)
    aw_streams = [_wstream(inputs[k], cfg.CT) for k in
                  ("L0_attn_w_key", "L0_attn_w_value", "L0_attn_w_rec",
                   "L0_attn_w_out")]

    in_maps = []
    for r in range(cfg.NCORES):
        xc = xf[r * cfg.TCORE:(r + 1) * cfg.TCORE]
        xTc = np.ascontiguousarray(xc.T).reshape(cfg.CT, 128, cfg.TCORE)
        seq_start = (r * cfg.TCORE) % cfg.SEQ == 0
        im = {
            "xT": xTc,
            "vecs": _vecs_for_core(
                inputs["L0_attn_time_first"], inputs["L0_attn_time_mix_k"],
                inputs["L0_attn_time_mix_v"], inputs["L0_attn_time_mix_r"],
                inputs["L0_attn_xx"], inputs["L0_attn_aa"],
                inputs["L0_attn_bb"], inputs["L0_attn_pp"], cfg, seq_start),
            "gidx": _core_slabs(ghost, cfg, r),
        }
        for m in range(4):
            im[f"aw{m}"] = aw_streams[m]
        in_maps.append(im)

    nc = build_program(cfg, lut)
    res = run_bass_kernel_spmd(nc, in_maps, core_ids=list(range(cfg.NCORES)))

    outs = []
    for r in range(cfg.NCORES):
        o = np.asarray(res.results[r]["outT"], dtype=np.float32)
        outs.append(np.ascontiguousarray(o.reshape(C, cfg.TCORE).T))
    return np.concatenate(outs, axis=0).reshape(B, T, C)


def kernel(**inputs):
    return run(CFG_FULL, inputs)


# version 18
# speedup vs baseline: 1.0040x; 1.0040x over previous
import contextlib
import numpy as np
import ml_dtypes

import concourse.bass as bass
import concourse.mybir as mybir
from concourse.bass_utils import run_bass_kernel_spmd

EMIT_LOG = []
F32 = mybir.dt.float32
BF16 = mybir.dt.bfloat16
I32 = mybir.dt.int32
AF = mybir.ActivationFunctionType
ALU = mybir.AluOpType
BF = ml_dtypes.bfloat16


class Cfg:
    def __init__(self, C, I, TCORE, CHUNK, SEQ, NCORES, use_collective=True):
        self.C = C
        self.I = I
        self.TCORE = TCORE
        self.CHUNK = CHUNK
        self.SEQ = SEQ
        self.NCORES = NCORES
        self.CT = C // 128
        self.IT = I // 128
        self.NCH = TCORE // CHUNK
        self.DS = -(-I // C)
        self.IPAD = self.DS * C
        self.S_REAL = 2 * self.IT + self.CT * self.DS
        self.SPC = -(-self.S_REAL // NCORES)
        self.S_PAD = self.SPC * NCORES
        self.use_collective = use_collective
        if not use_collective:
            self.SPC = self.S_PAD


CFG_FULL = Cfg(C=2048, I=5504, TCORE=2048, CHUNK=512, SEQ=4096, NCORES=8)

V_TF, V_TMK, V_TMV, V_TMR, V_XX, V_AA, V_BB, V_PP = range(8)


def build_program(cfg, lut_vals, debug=False):
    C, I, CT, IT = cfg.C, cfg.I, cfg.CT, cfg.IT
    CH, NCH = cfg.CHUNK, cfg.NCH
    DS = cfg.DS
    nc = bass.Bass("TRN2", target_bir_lowering=False, debug=False)

    xT = nc.dram_tensor("xT", [CT, 128, cfg.TCORE], F32, kind="ExternalInput").ap()
    vecs = nc.dram_tensor("vecs", [128, 8 * CT], F32, kind="ExternalInput").ap()
    gidx = nc.dram_tensor("gidx", [cfg.SPC, 128, C], I32, kind="ExternalInput").ap()
    aw = [nc.dram_tensor(f"aw{m}", [CT, 128, C], F32, kind="ExternalInput").ap()
          for m in range(4)]
    outT = nc.dram_tensor("outT", [CT, 128, cfg.TCORE], F32, kind="ExternalOutput").ap()
    dbg = {}
    if debug:
        for nm, w, dt_ in (("d_ln1", CT * CH, BF16), ("d_h", IT * CH, BF16),
                           ("d_x1", CT * CH, F32), ("d_kvr", 3 * CH, F32),
                           ("d_x2", CT * CH, F32), ("d_slab", C, BF16),
                           ("d_aw", C, BF16), ("d_rwkv", CT * CH, BF16),
                           ("d_bias", 3 * CT, F32), ("d_ckv", 3 * CT, F32),
                           ("d_vt", 8 * CT, F32), ("d_ln3", CT * CH, BF16)):
            dbg[nm] = nc.dram_tensor(nm, [128, w], dt_, kind="ExternalOutput").ap()

    GE_PAD = cfg.S_PAD * 128 * C
    if cfg.use_collective:
        cc_in = nc.dram_tensor("cc_in", [cfg.SPC * 128 * C], BF16, kind="Internal").ap()
    cc_out = nc.dram_tensor("cc_out", [GE_PAD], BF16, kind="Internal").ap()
    acache = nc.dram_tensor("acache", [4, CT, 128, C], BF16, kind="Internal").ap()

    def slab_dst_off(s):
        return s * 128 * C

    def slab_ap(s):
        off = slab_dst_off(s)
        return cc_out[off:off + 128 * C].rearrange("(p f) -> p f", p=128)

    with contextlib.ExitStack() as st:
        def sb(name, shape, dt):
            return st.enter_context(nc.sbuf_tensor(name, shape, dt))

        xs = [sb(f"xs{b}", [128, CT * CH], F32) for b in range(2)]
        lnT = sb("lnT", [128, CT * CH], BF16)
        hT = sb("hT", [128, IT * CH], BF16)
        assert IT >= CT
        rwkv = hT[:, 0:CT * CH]
        wfs = [sb(f"wfs{b}", [128, max(I, 2 * C)], BF16) for b in range(2)]
        if CT * CH * 2 >= 2 * 2 * C:
            mi = [xs[1][:, b * C:(b + 1) * C].bitcast(I32) for b in range(2)]
        else:
            mi = [sb(f"mi{b}", [128, C], I32)[:] for b in range(2)]
        was = [sb(f"was{b}", [128, C], F32) for b in range(2)]
        wab = [sb(f"wab{b}", [128, C], BF16) for b in range(2)]
        mattmp = sb("mattmp", [128, C], BF16)
        mato = [sb(f"mato{b}", [128, C], BF16) for b in range(2)]
        assert C == 4 * CH
        kv_arena = sb("kv_arena", [128, 2 * CH], F32)
        k_s = kv_arena[:, 0:CH]
        v_s = kv_arena[:, CH:2 * CH]
        matbf = kv_arena[:].bitcast(BF16)
        r_s = sb("r_s", [128, CH], F32)
        tmpA = sb("tmpA", [128, CH], F32)
        tmpC = sb("tmpC", [128, CH], F32)
        sq = [sb(f"sq{b}", [128, CH], F32) for b in range(2)]
        tmpB = sq[1]
        tmpD = sq[0]
        sg = sb("sg", [128, CH], BF16)
        vt = sb("vt", [128, 8 * CT], F32)
        ckv = sb("ckv", [128, 3 * CT], F32)
        bias = sb("bias", [128, 3 * CT], F32)
        rows = sb("rows", [1, 5 * CH], F32)
        fx = sb("fx", [128, max(8, CT)], F32)
        ones1 = sb("ones1", [1, 128], F32)
        onesk = sb("onesk", [128, 1], F32)
        onesb = sb("onesb", [128, 1], BF16)
        sqb = [sq[b][:].bitcast(BF16) for b in range(2)]

        PS = [st.enter_context(nc.psum_tensor(f"ps{b}", [128, 512], F32))
              for b in range(8)]
        PS_G, PS_U, PS_D0, PS_D1, PS_S1, PS_S2, PS_BA, PS_BB = PS

        class Sem:
            def __init__(self, name):
                self.h = st.enter_context(nc.semaphore(name))
                self.n = 0
                self.name = name

        sPE, sDV, sAC = Sem("sPE"), Sem("sDV"), Sem("sAC")
        sLX, sSX = Sem("sLX"), Sem("sSX")
        sWF = [Sem("sWF0"), Sem("sWF1")]
        sWA = [Sem("sWA0"), Sem("sWA1")]
        sWB = [Sem("sWB0"), Sem("sWB1")]
        sMI = [Sem("sMI0"), Sem("sMI1")]
        sSW = Sem("sSW")
        sAWR = Sem("sAWR")
        sCC = Sem("sCC")
        sLV = Sem("sLV")
        sONE = Sem("sONE")

        last_wait = {}

        def W(eng, ekey, sem, val):
            if val <= 0:
                return
            assert val <= sem.n, (sem.name, val, sem.n)
            if last_wait.get((ekey, sem.name), 0) >= val:
                return
            eng.wait_ge(sem.h, val)
            last_wait[(ekey, sem.name)] = val

        def inc(instr, sem, amt=1):
            instr.then_inc(sem.h, amt)
            sem.n += amt
            return sem.n

        block = st.enter_context(nc.Block())

        bank_free = {id(p): (sDV, 0) for p in PS}
        bank_ready = {}

        slot_free_wf = [(sPE, 0), (sPE, 0)]
        slot_free_wa = [(sPE, 0), (sPE, 0)]
        x_ready = [[(sDV, 0)] * CT for _ in range(NCH)]
        ln_free = (sPE, 0)
        kvr_free = (sDV, 0)
        sg_free = (sDV, 0)
        rwkv_cnt = 0

        @block.gpsimd
        def _(g):
            inc(g.dma_start(out=vt[:], in_=vecs), sLV, 16)
            for ct in range(CT):
                inc(g.dma_start(out=xs[0][:, ct * CH:(ct + 1) * CH],
                                in_=xT[ct, :, 0:CH]), sLX, 16)

        @block.vector
        def _(v):
            W(v, "V", sLV, 16)
            for m, vrow in enumerate((V_TMK, V_TMV, V_TMR)):
                tm = vt[:, vrow * CT:(vrow + 1) * CT]
                xxv = vt[:, V_XX * CT:(V_XX + 1) * CT]
                dst = ckv[:, m * CT:(m + 1) * CT]
                v.tensor_tensor(fx[:, 0:CT], xxv, tm, ALU.mult)
                v.drain()
                v.tensor_tensor(dst, xxv, fx[:, 0:CT], ALU.subtract)
                v.drain()
            inc(v.drain(), sDV, 1)
        ck_ms = sDV.n

        @block.gpsimd
        def _(g):
            g.memset(ones1[:], 1.0)
            g.memset(onesb[:], 1.0)
            inc(g.memset(onesk[:], 1.0), sONE, 1)
        ones_ms = sONE.n

        bias_ms = [0]

        def conv_slab(j):
            m, ot = j // CT, j % CT
            slot = j % 2

            @block.sync
            def _(y):
                fs, fv = slot_free_wa[slot]
                W(y, "Y", fs, fv)
                inc(y.dma_start(out=was[slot][:], in_=aw[m][ot]), sWA[slot], 16)

            if m < 3:
                @block.tensor
                def _(t):
                    W(t, "T", sWA[slot], sWA[slot].n)
                    W(t, "T", sDV, ck_ms)
                    for ct in range(CT):
                        mm = t.matmul(PS_D1[:, j:j + 1],
                                      was[slot][:, ct * 128:(ct + 1) * 128],
                                      ckv[:, m * CT + ct:m * CT + ct + 1],
                                      start=(ct == 0), stop=(ct == CT - 1))
                    inc(mm, sPE, 1)

            @block.vector
            def _(v):
                W(v, "V", sLV, 16)
                if m < 3:
                    W(v, "V", sPE, sPE.n)
                else:
                    W(v, "V", sWA[slot], sWA[slot].n)
                W(v, "V", sAWR, 16 * j)
                if m < 3:
                    for ct in range(CT):
                        v.tensor_scalar(wab[1][:, ct * 128:(ct + 1) * 128],
                                        was[slot][:, ct * 128:(ct + 1) * 128],
                                        vt[:, (V_TMK + m) * CT + ct:
                                           (V_TMK + m) * CT + ct + 1],
                                        None, ALU.mult)
                else:
                    v.tensor_copy(wab[1][:], was[slot][:])
                if j == 3 * CT - 1:
                    v.tensor_tensor(bias[:, 0:CT], PS_D1[:, 0:CT],
                                    vt[:, V_TF * CT:(V_TF + 1) * CT], ALU.add)
                    v.tensor_copy(bias[:, CT:3 * CT], PS_D1[:, CT:3 * CT])
                inc(v.drain(), sDV, 1)
                slot_free_wa[slot] = (sDV, sDV.n)
                if j == 3 * CT - 1:
                    bank_free[id(PS_D1)] = (sDV, sDV.n)
            if j == 3 * CT - 1:
                bias_ms[0] = sDV.n
            cms = sDV.n

            @block.sync
            def _(y):
                W(y, "Y", sDV, cms)
                inc(y.dma_start(out=acache[m, ot], in_=wab[1][:]), sAWR, 16)

        def conv_finish():
            @block.sync
            def _(y):
                W(y, "Y", sAWR, sAWR.n)



        mat_dve_done = [0] * max(cfg.SPC, 1)
        a1_hooks = []
        conv_a1_queue = list(range(4 * CT))

        def _conv_distributor(k):
            n = -(-len(range(4 * CT)) // cfg.SPC) if cfg.SPC else 0
            for _ in range(max(4, n)):
                if conv_a1_queue:
                    conv_slab(conv_a1_queue.pop(0))
        a1_hooks.append(_conv_distributor)


        @block.gpsimd
        def _(g):
            for k0 in range(min(2, cfg.SPC)):
                inc(g.dma_start(out=mi[k0][:, 0:C], in_=gidx[k0]), sMI[k0], 16)

        for k in range(cfg.SPC):
            slot = k % 2

            @block.vector
            def _(v, k=k, slot=slot):
                W(v, "V", sMI[slot], 16 * (k // 2 + 1))
                if k >= 2:
                    W(v, "V", sSW, 16 * (k - 1))
                v.tensor_copy(matbf[:], mi[slot][:, 0:C])
                mo = mato[k % 2]
                v.tensor_scalar(mo[:], matbf[:], 0.0, float(lut_vals[0]),
                                ALU.is_equal, ALU.mult)
                for j in range(1, 16):
                    v.tensor_scalar(mattmp[:], matbf[:], float(j),
                                    float(lut_vals[j]), ALU.is_equal, ALU.mult)
                    lastv = v.tensor_tensor(mo[:], mo[:], mattmp[:], ALU.add)
                inc(lastv, sDV, 1)
                mat_dve_done[k] = sDV.n

            @block.gpsimd
            def _(g, k=k, slot=slot):
                W(g, "G", sDV, mat_dve_done[k])
                if k + 2 < cfg.SPC:
                    inc(g.dma_start(out=mi[slot][:, 0:C], in_=gidx[k + 2]),
                        sMI[slot], 16)
                if cfg.use_collective:
                    dst = cc_in[k * 128 * C:(k + 1) * 128 * C].rearrange(
                        "(p f) -> p f", p=128)
                else:
                    dst = cc_out[k * 128 * C:(k + 1) * 128 * C].rearrange(
                        "(p f) -> p f", p=128)
                inc(g.dma_start(out=dst, in_=mato[k % 2][:]), sSW, 16)
                if cfg.use_collective:
                    W(g, "G", sSW, 16 * (k + 1))
                    gsz = cfg.NCORES * 128 * C
                    inc(g.collective_compute(
                        "AllGather", ALU.bypass,
                        replica_groups=[list(range(cfg.NCORES))],
                        ins=[cc_in[k * 128 * C:(k + 1) * 128 * C]
                             .rearrange("(r c) -> r c", c=C).opt()],
                        outs=[cc_out[k * gsz:(k + 1) * gsz]
                              .rearrange("(r c) -> r c", c=C).opt()],
                    ), sCC, 1)
                else:
                    inc(g.nop(), sCC, 1)
            for fn in a1_hooks:
                fn(k)

        @block.gpsimd
        def _(g):
            W(g, "G", sSW, sSW.n)


        def make_stats_hook(ch, xset, same_phase, bf16_ok=True):
            st_state = {"pe": [0, 0], "ms": 0}

            def hook(ct, xr=None):
                b = ct % 2
                xrs, xrv = xr if (same_phase and xr is not None) \
                    else x_ready[ch][ct]

                @block.scalar
                def _(a, ct=ct, b=b, xrs=xrs, xrv=xrv):
                    W(a, "A", xrs, xrv)
                    if ct == 0:
                        W(a, "A", sLX, 16 * CT * (ch + 1))
                    if ct >= 2:
                        W(a, "A", sPE, st_state["pe"][b])
                    if bf16_ok:
                        a.activation(lnT[:, ct * CH:(ct + 1) * CH],
                                     xs[xset][:, ct * CH:(ct + 1) * CH],
                                     AF.Identity)
                        inc(a.activation(sqb[b][:, 0:CH],
                                         xs[xset][:, ct * CH:(ct + 1) * CH],
                                         AF.Square), sAC, 1)
                    else:
                        inc(a.activation(sq[b][:, 0:CH],
                                         xs[xset][:, ct * CH:(ct + 1) * CH],
                                         AF.Square), sAC, 1)

                @block.tensor
                def _(t, ct=ct, b=b, xrs=xrs, xrv=xrv):
                    W(t, "T", sAC, sAC.n)
                    W(t, "T", xrs, xrv)
                    if ct == 0:
                        W(t, "T", sLX, 16 * CT * (ch + 1))
                        W(t, "T", sONE, ones_ms)
                        for p in (PS_S1, PS_S2):
                            bs, bv = bank_free[id(p)]
                            W(t, "T", bs, bv)
                    if bf16_ok:
                        t.matmul(PS_S1[0:1, 0:CH], onesb[:],
                                 lnT[:, ct * CH:(ct + 1) * CH],
                                 start=(ct == 0), stop=(ct == CT - 1))
                        mm = t.matmul(PS_S2[0:1, 0:CH], onesb[:],
                                      sqb[b][:, 0:CH],
                                      start=(ct == 0), stop=(ct == CT - 1))
                    else:
                        t.matmul(PS_S1[0:1, 0:CH], onesk[:],
                                 xs[xset][:, ct * CH:(ct + 1) * CH],
                                 start=(ct == 0), stop=(ct == CT - 1))
                        mm = t.matmul(PS_S2[0:1, 0:CH], onesk[:], sq[b][:, 0:CH],
                                      start=(ct == 0), stop=(ct == CT - 1))
                    inc(mm, sPE, 1)
                    st_state["pe"][b] = sPE.n
                    if ct == CT - 1:
                        st_state["ms"] = sPE.n
            return hook, st_state

        def emit_ln(ch, xset, tag, stats_done=None):
            nonlocal ln_free
            sl = slice(0, CH)

            if stats_done is None:
                hook, st_state = make_stats_hook(ch, xset, same_phase=False,
                                                 bf16_ok=False)
                for ct in range(CT):
                    hook(ct)
                stats_ms = st_state["ms"]
            else:
                stats_ms = stats_done
            bank_ready[id(PS_S1)] = (sPE, stats_ms)
            bank_ready[id(PS_S2)] = (sPE, stats_ms)

            m_row = rows[0:1, 0:CH]
            v_row = rows[0:1, CH:2 * CH]
            a_row = rows[0:1, 2 * CH:3 * CH]
            b_row = rows[0:1, 3 * CH:4 * CH]
            t_row = rows[0:1, 4 * CH:5 * CH]
            t2_row = v_row

            @block.vector
            def _(v):
                W(v, "V", sPE, stats_ms)
                v.tensor_scalar(m_row, PS_S1[0:1, sl], 1.0 / C, None, ALU.mult)
                v.tensor_scalar(v_row, PS_S2[0:1, sl], 1.0 / C, 1e-5,
                                ALU.mult, ALU.add)
                v.drain()
                v.tensor_tensor(t_row, m_row, m_row, ALU.mult)
                v.drain()
                v.tensor_tensor(v_row, v_row, t_row, ALU.subtract)
                inc(v.drain(), sDV, 1)
                bank_free[id(PS_S1)] = (sDV, sDV.n)
                bank_free[id(PS_S2)] = (sDV, sDV.n)
                EMIT_LOG.append(("rows_read", ch, tag, stats_ms, sDV.n))

            @block.scalar
            def _(a):
                W(a, "A", sDV, sDV.n)
                a.activation(t_row, v_row, AF.Ln)
                inc(a.drain(), sAC, 1)

            @block.vector
            def _(v):
                W(v, "V", sAC, sAC.n)
                v.tensor_scalar(t2_row, t_row, -0.5, None, ALU.mult)
                inc(v.drain(), sDV, 1)

            @block.scalar
            def _(a):
                W(a, "A", sDV, sDV.n)
                a.activation(a_row, t2_row, AF.Exp)
                inc(a.drain(), sAC, 1)

            @block.vector
            def _(v):
                W(v, "V", sAC, sAC.n)
                v.tensor_tensor(t_row, m_row, a_row, ALU.mult)
                v.drain()
                v.tensor_scalar(b_row, t_row, -1.0, None, ALU.mult)
                inc(v.drain(), sDV, 1)
            rows_ms = sDV.n

            @block.tensor
            def _(t):
                W(t, "T", sDV, rows_ms)
                for p in (PS_BA, PS_BB):
                    bs, bv = bank_free[id(p)]
                    W(t, "T", bs, bv)
                t.matmul(PS_BA[:, sl], ones1[:], a_row, start=True, stop=True)
                mm = t.matmul(PS_BB[:, sl], ones1[:], b_row, start=True, stop=True)
                inc(mm, sPE, 1)
                bank_ready[id(PS_BA)] = (sPE, sPE.n)
                bank_ready[id(PS_BB)] = (sPE, sPE.n)
            bc_ms = sPE.n

            ln_tiles = [0] * CT

            @block.vector
            def _(v):
                W(v, "V", sPE, bc_ms)
                fs, fv = ln_free
                W(v, "V", fs, fv)
                for ct in range(CT):
                    xap = xs[xset][:, ct * CH:(ct + 1) * CH]
                    v.tensor_tensor(tmpA[:, sl], xap, PS_BA[:, sl], ALU.mult)
                    inc(v.tensor_tensor(lnT[:, ct * CH:(ct + 1) * CH],
                                        tmpA[:, sl], PS_BB[:, sl], ALU.add),
                        sDV, 1)
                    ln_tiles[ct] = sDV.n
                bank_free[id(PS_BA)] = (sDV, sDV.n)
                bank_free[id(PS_BB)] = (sDV, sDV.n)
            return ln_tiles

        fs_cnt = [0]

        def ffn_slab_load(src_slabs, slot):
            if cfg.use_collective:
                need = max(s // cfg.NCORES for s, _ in src_slabs) + 1
            else:
                need = max(s for s, _ in src_slabs) + 1

            @block.sync
            def _(y):
                W(y, "Y", sCC, need)
                fsm, fvv = slot_free_wf[slot]
                W(y, "Y", fsm, fvv)
                for s, coff in src_slabs:
                    w = min(C, I - coff)
                    inc(y.dma_start(out=wfs[slot][:, coff:coff + w],
                                    in_=slab_ap(s)[:, 0:w]), sWF[slot], 16)

        conv_queue = []
        assert 3 * CT <= 2 * IT

        def emit_ffn(ch, xset, ln_ms, conv_hook=None, tail_hook=None):
            nonlocal sg_free
            sl = slice(0, CH)
            h_ms = 0
            silu_ms = {}
            for it in range(IT):
                if conv_hook is not None and conv_queue:
                    for _ in range(2):
                        if conv_queue:
                            conv_hook(conv_queue.pop(0))
                for half, base in ((0, 0), (1, IT)):
                    slot = fs_cnt[0] % 2
                    fs_cnt[0] += 1
                    ffn_slab_load([(base + it, 0)], slot)

                    @block.tensor
                    def _(t, it=it, half=half, slot=slot):
                        W(t, "T", sWF[slot], sWF[slot].n)
                        p = PS_G if half == 0 else PS_U
                        bs, bv = bank_free[id(p)]
                        W(t, "T", bs, bv)
                        for ht in range(CT):
                            W(t, "T", sDV, ln_ms[ht])
                            mm = t.matmul(p[:, sl],
                                          wfs[slot][:, ht * 128:(ht + 1) * 128],
                                          lnT[:, ht * CH:(ht + 1) * CH],
                                          start=(ht == 0), stop=(ht == CT - 1))
                        inc(mm, sPE, 1)
                        bank_ready[id(p)] = (sPE, sPE.n)
                        slot_free_wf[slot] = (sPE, sPE.n)

                @block.scalar
                def _(a, it=it):
                    rs, rv = bank_ready[id(PS_G)]
                    W(a, "A", rs, rv)
                    fs2, fv2 = sg_free
                    W(a, "A", fs2, fv2)
                    inc(a.activation(sg[:, sl], PS_G[:, sl], AF.Silu), sAC, 1)
                    silu_ms[it] = sAC.n
                    bank_free[id(PS_G)] = (sAC, sAC.n)

                @block.vector
                def _(v, it=it):
                    W(v, "V", sAC, silu_ms[it])
                    rs, rv = bank_ready[id(PS_U)]
                    W(v, "V", rs, rv)
                    inc(v.tensor_tensor(hT[:, it * CH:(it + 1) * CH], sg[:, sl],
                                        PS_U[:, sl], ALU.mult), sDV, 1)
                    bank_free[id(PS_U)] = (sDV, sDV.n)
                    sg_free = (sDV, sDV.n)
                h_ms = sDV.n

            res_list = [None] * CT
            for ot in range(CT):
                if ot >= 2 and tail_hook is not None:
                    tail_hook(ot - 2, res_list[ot - 2])
                slot = fs_cnt[0] % 2
                fs_cnt[0] += 1
                ffn_slab_load([(2 * IT + ot * DS + j, j * C) for j in range(DS)],
                              slot)
                p = PS_D0 if ot % 2 == 0 else PS_D1

                @block.tensor
                def _(t, ot=ot, slot=slot, p=p):
                    W(t, "T", sWF[slot], sWF[slot].n)
                    W(t, "T", sDV, h_ms)
                    bs, bv = bank_free[id(p)]
                    W(t, "T", bs, bv)
                    for it in range(IT):
                        mm = t.matmul(p[:, sl],
                                      wfs[slot][:, it * 128:(it + 1) * 128],
                                      hT[:, it * CH:(it + 1) * CH],
                                      start=(it == 0), stop=(it == IT - 1))
                    inc(mm, sPE, 1)
                    bank_ready[id(p)] = (sPE, sPE.n)
                    slot_free_wf[slot] = (sPE, sPE.n)

                @block.vector
                def _(v, ot=ot, p=p):
                    rs, rv = bank_ready[id(p)]
                    W(v, "V", rs, rv)
                    inc(v.scalar_tensor_tensor(
                        xs[xset][:, ot * CH:(ot + 1) * CH], p[:, sl], 0.5,
                        xs[xset][:, ot * CH:(ot + 1) * CH], ALU.mult, ALU.add),
                        sDV, 1)
                    bank_free[id(p)] = (sDV, sDV.n)
                res_list[ot] = (sDV, sDV.n)
            if tail_hook is not None:
                for ct in range(CT - 2, CT):
                    tail_hook(ct, res_list[ct])
            return res_list

        a_cnt = [4 * CT]

        def attn_slab_load(m, ot, slot):
            @block.sync
            def _(y):
                fsm, fvv = slot_free_wa[slot]
                W(y, "Y", fsm, fvv)
                inc(y.dma_start(out=wab[slot][:], in_=acache[m, ot]), sWA[slot], 16)

        def emit_attn(ch, xset, ln_ms, tail_hook=None):
            nonlocal kvr_free, rwkv_cnt, ln_free
            sl = slice(0, CH)
            chain_ms = 0
            for ot in range(CT):
                psum_of = {0: PS_G, 1: PS_U, 2: PS_D0}
                for m in range(3):
                    slot = a_cnt[0] % 2
                    a_cnt[0] += 1
                    attn_slab_load(m, ot, slot)
                    p = psum_of[m]

                    @block.tensor
                    def _(t, m=m, ot=ot, slot=slot, p=p):
                        W(t, "T", sWA[slot], sWA[slot].n)
                        bs, bv = bank_free[id(p)]
                        W(t, "T", bs, bv)
                        for ct in range(CT):
                            W(t, "T", sDV, ln_ms[ct])
                            mm = t.matmul(p[:, sl],
                                          wab[slot][:, ct * 128:(ct + 1) * 128],
                                          lnT[:, ct * CH:(ct + 1) * CH],
                                          start=(ct == 0), stop=(ct == CT - 1))
                        inc(mm, sPE, 1)
                        bank_ready[id(p)] = (sPE, sPE.n)
                        slot_free_wa[slot] = (sPE, sPE.n)

                @block.scalar
                def _(a, ot=ot):
                    W(a, "A", sDV, bias_ms[0])
                    fs2, fv2 = kvr_free
                    W(a, "A", fs2, fv2)
                    for m, (p, dst) in enumerate(((PS_G, k_s), (PS_U, v_s),
                                                  (PS_D0, r_s))):
                        rs, rv = bank_ready[id(p)]
                        W(a, "A", rs, rv)
                        fn = AF.Sigmoid if m == 2 else AF.Identity
                        lasta = a.activation(
                            dst[:, sl], p[:, sl], fn,
                            bias=bias[:, m * CT + ot:m * CT + ot + 1],
                            scale=1.0)
                    inc(lasta, sAC, 1)
                    bank_free[id(PS_G)] = (sAC, sAC.n)
                    bank_free[id(PS_U)] = (sAC, sAC.n)
                    bank_free[id(PS_D0)] = (sAC, sAC.n)
                evac_ms = sAC.n

                @block.vector
                def _(v, ot=ot):
                    W(v, "V", sAC, evac_ms)
                    v.tensor_scalar(tmpA[:, sl], k_s[:, sl], 0.0, None, ALU.min)
                    lastv = v.tensor_scalar(tmpC[:, sl], k_s[:, sl], -1.0, 0.0,
                                            ALU.mult, ALU.max)
                    inc(lastv, sDV, 1)
                m2_ms = sDV.n

                @block.scalar
                def _(a, ot=ot):
                    W(a, "A", sDV, m2_ms)
                    a.activation(tmpB[:, sl], tmpA[:, sl], AF.Exp)
                    lasta = a.activation(tmpD[:, sl], tmpC[:, sl], AF.Exp)
                    inc(lasta, sAC, 1)
                exp_ms = sAC.n
                if debug and ch == 0 and ot == 0:
                    @block.gpsimd
                    def _(g):
                        W(g, "G", sAC, evac_ms)
                        inc(g.dma_start(out=dbg["d_kvr"][:, 0:CH],
                                        in_=k_s[:, sl]), sSX, 16)
                        inc(g.dma_start(out=dbg["d_kvr"][:, CH:2 * CH],
                                        in_=v_s[:, sl]), sSX, 16)
                        inc(g.dma_start(out=dbg["d_kvr"][:, 2 * CH:3 * CH],
                                        in_=r_s[:, sl]), sSX, 16)

                @block.vector
                def _(v, ot=ot):
                    W(v, "V", sAC, exp_ms)
                    v.tensor_tensor(tmpB[:, sl], tmpB[:, sl], v_s[:, sl], ALU.mult)
                    v.tensor_tensor(tmpB[:, sl], tmpB[:, sl], tmpD[:, sl], ALU.mult)
                    inc(v.tensor_tensor(rwkv[:, ot * CH:(ot + 1) * CH],
                                        tmpB[:, sl], r_s[:, sl], ALU.mult),
                        sDV, 1)

                if ch == 0:
                    f1 = fx[:, 0:1]; f2 = fx[:, 1:2]; f3 = fx[:, 2:3]
                    f4 = fx[:, 3:4]; f5 = fx[:, 4:5]; f6 = fx[:, 5:6]
                    f7 = fx[:, 6:7]; f8 = fx[:, 7:8]
                    ppc = vt[:, V_PP * CT + ot:V_PP * CT + ot + 1]
                    aac = vt[:, V_AA * CT + ot:V_AA * CT + ot + 1]
                    bbc = vt[:, V_BB * CT + ot:V_BB * CT + ot + 1]

                    @block.vector
                    def _(v, ot=ot):
                        v.tensor_tensor(f1, ppc, k_s[:, 0:1], ALU.max)
                        v.drain()
                        v.tensor_tensor(f2, ppc, f1, ALU.subtract)
                        v.tensor_tensor(f3, k_s[:, 0:1], f1, ALU.subtract)
                        inc(v.drain(), sDV, 1)

                    @block.scalar
                    def _(a, ot=ot):
                        W(a, "A", sDV, sDV.n)
                        a.activation(f6, f2, AF.Exp)
                        a.activation(f7, f3, AF.Exp)
                        inc(a.drain(), sAC, 1)

                    @block.vector
                    def _(v, ot=ot):
                        W(v, "V", sAC, sAC.n)
                        v.tensor_tensor(f4, f6, aac, ALU.mult)
                        v.tensor_tensor(f5, f7, v_s[:, 0:1], ALU.mult)
                        v.drain()
                        v.tensor_tensor(f4, f4, f5, ALU.add)
                        v.tensor_tensor(f2, f6, bbc, ALU.mult)
                        v.drain()
                        v.tensor_tensor(f2, f2, f7, ALU.add)
                        v.drain()
                        v.reciprocal(f8, f2)
                        v.drain()
                        v.tensor_tensor(f4, f4, f8, ALU.mult)
                        v.drain()
                        v.tensor_tensor(rwkv[:, ot * CH:ot * CH + 1], f4,
                                        r_s[:, 0:1], ALU.mult)
                        inc(v.drain(), sDV, 1)

                chain_ms = sDV.n
                kvr_free = (sDV, chain_ms)

            res_list = [None] * CT
            for ot in range(CT):
                if ot >= 2 and tail_hook is not None:
                    tail_hook(ot - 2, res_list[ot - 2])
                slot = a_cnt[0] % 2
                a_cnt[0] += 1
                attn_slab_load(3, ot, slot)
                p = PS_D1 if ot % 2 == 0 else PS_D0

                @block.tensor
                def _(t, ot=ot, slot=slot, p=p):
                    W(t, "T", sWA[slot], sWA[slot].n)
                    W(t, "T", sDV, chain_ms)
                    bs, bv = bank_free[id(p)]
                    W(t, "T", bs, bv)
                    for ct in range(CT):
                        mm = t.matmul(p[:, sl],
                                      wab[slot][:, ct * 128:(ct + 1) * 128],
                                      rwkv[:, ct * CH:(ct + 1) * CH],
                                      start=(ct == 0), stop=(ct == CT - 1))
                    inc(mm, sPE, 1)
                    bank_ready[id(p)] = (sPE, sPE.n)
                    slot_free_wa[slot] = (sPE, sPE.n)
                if ot == CT - 1:
                    ln_free = (sPE, sPE.n)

                @block.vector
                def _(v, ot=ot, p=p):
                    rs, rv = bank_ready[id(p)]
                    W(v, "V", rs, rv)
                    inc(v.scalar_tensor_tensor(
                        xs[xset][:, ot * CH:(ot + 1) * CH], p[:, sl], 1.0,
                        xs[xset][:, ot * CH:(ot + 1) * CH], ALU.mult, ALU.add),
                        sDV, 1)
                    bank_free[id(p)] = (sDV, sDV.n)
                res_list[ot] = (sDV, sDV.n)
            if tail_hook is not None:
                for ct in range(CT - 2, CT):
                    tail_hook(ct, res_list[ct])
            return res_list

        def dump(name, src_ap, width, wait_sem, wait_val):
            if not debug:
                return
            @block.gpsimd
            def _(g):
                W(g, "G", wait_sem, wait_val)
                inc(g.dma_start(out=dbg[name], in_=src_ap), sSX, 16)

        def dump_dram(name, src_ap, width, wait_sem, wait_val):
            if not debug:
                return
            @block.gpsimd
            def _(g):
                W(g, "G", wait_sem, wait_val)
                inc(g.dma_start(out=dbg[name], in_=src_ap), sSX, 16)

        if debug:
            dump_dram("d_slab", slab_ap(0), C, sCC, 1)
            dump("d_ckv", ckv[:], 3 * CT, sDV, ck_ms)
            dump("d_vt", vt[:], 8 * CT, sLV, 16)

        store_ms = [0] * (NCH + 2)

        def emit_xload(ch):
            if ch == 0 or ch >= NCH:
                return

            @block.gpsimd
            def _(g, ch=ch):
                if ch >= 2:
                    W(g, "G", sSX, store_ms[ch - 2])
                for ct in range(CT):
                    inc(g.dma_start(
                        out=xs[ch % 2][:, ct * CH:(ct + 1) * CH],
                        in_=xT[ct, :, ch * CH:(ch + 1) * CH]), sLX, 16)

        def emit_ffn1(ch, stats_done=None, tail_hook=None):
            xset = ch % 2
            ln1 = emit_ln(ch, xset, "ffn1", stats_done=stats_done)
            if ch == 0:
                dump("d_ln1", lnT[:], CT * CH, sDV, ln1[-1])
            r1 = emit_ffn(ch, xset, ln1, tail_hook=tail_hook)
            x_ready[ch] = r1
            if ch == 0:
                dump("d_h", hT[:], IT * CH, sDV, r1[-1][1])
                dump("d_x1", xs[xset][:], CT * CH, sDV, r1[-1][1])

        def emit_rest(ch, pre_attn_stats=None, next_hook=None, use_hook2=True):
            xset = ch % 2
            lna = emit_ln(ch, xset, "attn", stats_done=pre_attn_stats)
            if use_hook2:
                hook2, st2 = make_stats_hook(ch, xset, same_phase=True)
            r2 = emit_attn(ch, xset, lna,
                           tail_hook=(hook2 if use_hook2 else None))
            x_ready[ch] = r2
            if ch == 0:
                dump("d_x2", xs[xset][:], CT * CH, sDV, r2[-1][1])
            ln2 = emit_ln(ch, xset, "ffn2",
                          stats_done=(st2["ms"] if use_hook2 else None))
            if ch == 0:
                dump("d_ln3", lnT[:], CT * CH, sDV, ln2[-1])
            r3 = emit_ffn(ch, xset, ln2, tail_hook=next_hook)
            x_ready[ch] = r3

            @block.gpsimd
            def _(g, ch=ch, xset=xset, r3=r3):
                W(g, "G", sDV, r3[-1][1])
                for ct in range(CT):
                    inc(g.dma_start(out=outT[ct, :, ch * CH:(ch + 1) * CH],
                                    in_=xs[xset][:, ct * CH:(ct + 1) * CH]),
                        sSX, 16)
            store_ms[ch] = sSX.n

        emit_xload(0)
        emit_xload(1)
        HOOKS = True
        HOOK2 = True
        emit_ffn1(0)
        hA0, stA0 = make_stats_hook(0, 0, same_phase=False)
        emit_ffn1(1, tail_hook=(hA0 if HOOKS else None))
        conv_finish()
        if debug:
            dump_dram("d_aw", acache[0, 0], C, sAWR, 16 * 1)
            dump("d_bias", bias[:], 3 * CT, sDV, bias_ms[0])
        hooks = {}
        if NCH > 1:
            hooks[1], hooks["s1"] = make_stats_hook(1, 1, same_phase=False)
        if NCH > 2:
            hooks[2], hooks["s2"] = make_stats_hook(2, 0, same_phase=False)
        if NCH > 3:
            hooks[3], hooks["s3"] = make_stats_hook(3, 1, same_phase=False)
        emit_rest(0, use_hook2=HOOK2, pre_attn_stats=(stA0["ms"] if HOOKS else None),
                  next_hook=(hooks.get(1) if HOOKS else None))
        emit_xload(2)
        emit_rest(1, use_hook2=HOOK2,
                  pre_attn_stats=(hooks["s1"]["ms"] if HOOKS and NCH > 1 else None),
                  next_hook=(hooks.get(2) if HOOKS else None))
        emit_xload(3)
        if NCH > 2:
            emit_ffn1(2, stats_done=(hooks["s2"]["ms"] if HOOKS else None))
            emit_rest(2, use_hook2=HOOK2, pre_attn_stats=None,
                      next_hook=(hooks.get(3) if HOOKS else None))
        if NCH > 3:
            emit_ffn1(3, stats_done=(hooks["s3"]["ms"] if HOOKS else None))
            emit_rest(3, use_hook2=HOOK2, pre_attn_stats=None, next_hook=None)

        @block.gpsimd
        def _(g):
            W(g, "G", sSX, store_ms[NCH - 1])

    return nc



def _wstream(w, CT):
    wT = np.ascontiguousarray(np.asarray(w, dtype=np.float32).T)
    C = wT.shape[0]
    s = wT.reshape(CT, 128, CT, 128).transpose(2, 1, 0, 3)
    return np.ascontiguousarray(s).reshape(CT, 128, C)


def _ghost_stream(gate_idx, up_idx, down_idx, cfg):
    C, I, CT, IT, DS = cfg.C, cfg.I, cfg.CT, cfg.IT, cfg.DS
    gT = np.ascontiguousarray(np.asarray(gate_idx, dtype=np.int32).T)
    uT = np.ascontiguousarray(np.asarray(up_idx, dtype=np.int32).T)
    dT = np.ascontiguousarray(np.asarray(down_idx, dtype=np.int32).T)
    g = gT.reshape(CT, 128, IT, 128).transpose(2, 1, 0, 3).reshape(IT, 128, C)
    u = uT.reshape(CT, 128, IT, 128).transpose(2, 1, 0, 3).reshape(IT, 128, C)
    d = dT.reshape(IT, 128, CT, 128).transpose(2, 1, 0, 3).reshape(CT, 128, I)
    dpad = np.zeros((CT, 128, cfg.IPAD), dtype=np.int32)
    dpad[:, :, :I] = d
    d_sl = dpad.reshape(CT, 128, DS, C).transpose(0, 2, 1, 3)
    flat = np.concatenate([g.reshape(-1), u.reshape(-1), d_sl.reshape(-1)])
    pad = np.zeros(cfg.S_PAD * 128 * C, dtype=np.int32)
    pad[:flat.size] = flat
    return pad.reshape(cfg.S_PAD, 128, C)


def _core_slabs(ghost, cfg, r):
    if not cfg.use_collective:
        return ghost
    ks = [cfg.NCORES * k + r for k in range(cfg.SPC)]
    return np.ascontiguousarray(ghost[ks])


def _vecs_for_core(tf, tmk, tmv, tmr, xx, aa, bb, pp, cfg, seq_start):
    C, CT = cfg.C, cfg.CT
    z = np.zeros(C, dtype=np.float32)
    neg = np.full(C, -1e38, dtype=np.float32)
    rows = [np.asarray(v, dtype=np.float32).reshape(C) for v in
            (tf, tmk, tmv, tmr, xx)]
    rows += [np.asarray(aa, np.float32).reshape(C) if seq_start else z,
             np.asarray(bb, np.float32).reshape(C) if seq_start else z,
             np.asarray(pp, np.float32).reshape(C) if seq_start else neg]
    v2 = np.stack(rows)
    return np.ascontiguousarray(
        v2.reshape(8, CT, 128).transpose(2, 0, 1).reshape(128, 8 * CT))


def run(cfg, inputs):
    x = np.asarray(inputs["L0_attn_x"], dtype=np.float32)
    B, T, C = x.shape
    xf = x.reshape(B * T, C)
    lut = [float(v) for v in np.asarray(inputs["L0_ffn_lut"], dtype=np.float32)]

    ghost = _ghost_stream(inputs["L0_ffn_gate_idx"], inputs["L0_ffn_up_idx"],
                          inputs["L0_ffn_down_idx"], cfg)
    aw_streams = [_wstream(inputs[k], cfg.CT) for k in
                  ("L0_attn_w_key", "L0_attn_w_value", "L0_attn_w_rec",
                   "L0_attn_w_out")]

    in_maps = []
    for r in range(cfg.NCORES):
        xc = xf[r * cfg.TCORE:(r + 1) * cfg.TCORE]
        xTc = np.ascontiguousarray(xc.T).reshape(cfg.CT, 128, cfg.TCORE)
        seq_start = (r * cfg.TCORE) % cfg.SEQ == 0
        im = {
            "xT": xTc,
            "vecs": _vecs_for_core(
                inputs["L0_attn_time_first"], inputs["L0_attn_time_mix_k"],
                inputs["L0_attn_time_mix_v"], inputs["L0_attn_time_mix_r"],
                inputs["L0_attn_xx"], inputs["L0_attn_aa"],
                inputs["L0_attn_bb"], inputs["L0_attn_pp"], cfg, seq_start),
            "gidx": _core_slabs(ghost, cfg, r),
        }
        for m in range(4):
            im[f"aw{m}"] = aw_streams[m]
        in_maps.append(im)

    nc = build_program(cfg, lut)
    res = run_bass_kernel_spmd(nc, in_maps, core_ids=list(range(cfg.NCORES)))

    outs = []
    for r in range(cfg.NCORES):
        o = np.asarray(res.results[r]["outT"], dtype=np.float32)
        outs.append(np.ascontiguousarray(o.reshape(C, cfg.TCORE).T))
    return np.concatenate(outs, axis=0).reshape(B, T, C)


def kernel(**inputs):
    return run(CFG_FULL, inputs)


# version 19 (final)
# speedup vs baseline: 1.0193x; 1.0152x over previous
import contextlib
import numpy as np
import ml_dtypes

import concourse.bass as bass
import concourse.mybir as mybir
from concourse.bass_utils import run_bass_kernel_spmd

EMIT_LOG = []
F32 = mybir.dt.float32
BF16 = mybir.dt.bfloat16
I32 = mybir.dt.int32
AF = mybir.ActivationFunctionType
ALU = mybir.AluOpType
BF = ml_dtypes.bfloat16


class Cfg:
    def __init__(self, C, I, TCORE, CHUNK, SEQ, NCORES, use_collective=True):
        self.C = C
        self.I = I
        self.TCORE = TCORE
        self.CHUNK = CHUNK
        self.SEQ = SEQ
        self.NCORES = NCORES
        self.CT = C // 128
        self.IT = I // 128
        self.NCH = TCORE // CHUNK
        self.DS = -(-I // C)
        self.IPAD = self.DS * C
        self.S_REAL = 2 * self.IT + self.CT * self.DS
        self.SPC = -(-self.S_REAL // NCORES)
        self.S_PAD = self.SPC * NCORES
        self.use_collective = use_collective
        if not use_collective:
            self.SPC = self.S_PAD


CFG_FULL = Cfg(C=2048, I=5504, TCORE=2048, CHUNK=512, SEQ=4096, NCORES=8)

V_TF, V_TMK, V_TMV, V_TMR, V_XX, V_AA, V_BB, V_PP = range(8)


def build_program(cfg, lut_vals, debug=False):
    C, I, CT, IT = cfg.C, cfg.I, cfg.CT, cfg.IT
    CH, NCH = cfg.CHUNK, cfg.NCH
    DS = cfg.DS
    nc = bass.Bass("TRN2", target_bir_lowering=False, debug=False)

    xT = nc.dram_tensor("xT", [CT, 128, cfg.TCORE], F32, kind="ExternalInput").ap()
    vecs = nc.dram_tensor("vecs", [128, 8 * CT], F32, kind="ExternalInput").ap()
    gidx = nc.dram_tensor("gidx", [cfg.SPC, 128, C], I32, kind="ExternalInput").ap()
    aw = [nc.dram_tensor(f"aw{m}", [CT, 128, C], F32, kind="ExternalInput").ap()
          for m in range(4)]
    outT = nc.dram_tensor("outT", [CT, 128, cfg.TCORE], F32, kind="ExternalOutput").ap()
    dbg = {}
    if debug:
        for nm, w, dt_ in (("d_ln1", CT * CH, BF16), ("d_h", IT * CH, BF16),
                           ("d_x1", CT * CH, F32), ("d_kvr", 3 * CH, F32),
                           ("d_x2", CT * CH, F32), ("d_slab", C, BF16),
                           ("d_aw", C, BF16), ("d_rwkv", CT * CH, BF16),
                           ("d_bias", 3 * CT, F32), ("d_ckv", 3 * CT, F32),
                           ("d_vt", 8 * CT, F32), ("d_ln3", CT * CH, BF16)):
            dbg[nm] = nc.dram_tensor(nm, [128, w], dt_, kind="ExternalOutput").ap()

    GE_PAD = cfg.S_PAD * 128 * C
    if cfg.use_collective:
        cc_in = nc.dram_tensor("cc_in", [cfg.SPC * 128 * C], BF16, kind="Internal").ap()
    cc_out = nc.dram_tensor("cc_out", [GE_PAD], BF16, kind="Internal").ap()
    acache = nc.dram_tensor("acache", [4, CT, 128, C], BF16, kind="Internal").ap()

    def slab_dst_off(s):
        return s * 128 * C

    def slab_ap(s):
        off = slab_dst_off(s)
        return cc_out[off:off + 128 * C].rearrange("(p f) -> p f", p=128)

    with contextlib.ExitStack() as st:
        def sb(name, shape, dt):
            return st.enter_context(nc.sbuf_tensor(name, shape, dt))

        xs = [sb(f"xs{b}", [128, CT * CH], F32) for b in range(2)]
        lnT = sb("lnT", [128, CT * CH], BF16)
        hT = sb("hT", [128, IT * CH], BF16)
        assert IT >= CT
        rwkv = hT[:, 0:CT * CH]
        wfs = [sb(f"wfs{b}", [128, max(I, 2 * C)], BF16) for b in range(2)]
        if CT * CH * 2 >= 2 * 2 * C:
            mi = [xs[1][:, b * C:(b + 1) * C].bitcast(I32) for b in range(2)]
        else:
            mi = [sb(f"mi{b}", [128, C], I32)[:] for b in range(2)]
        was = [sb(f"was{b}", [128, C], F32) for b in range(2)]
        wab = [sb(f"wab{b}", [128, C], BF16) for b in range(2)]
        mattmp = sb("mattmp", [128, C], BF16)
        mato = [sb(f"mato{b}", [128, C], BF16) for b in range(2)]
        assert C == 4 * CH
        kv_arena = sb("kv_arena", [128, 2 * CH], F32)
        k_s = kv_arena[:, 0:CH]
        v_s = kv_arena[:, CH:2 * CH]
        matbf = kv_arena[:].bitcast(BF16)
        r_s = sb("r_s", [128, CH], F32)
        tmpA = sb("tmpA", [128, CH], F32)
        tmpC = sb("tmpC", [128, CH], F32)
        sq = [sb(f"sq{b}", [128, CH], F32) for b in range(2)]
        tmpB = sq[1]
        tmpD = sq[0]
        sg = sb("sg", [128, CH], BF16)
        vt = sb("vt", [128, 8 * CT], F32)
        ckv = sb("ckv", [128, 3 * CT], F32)
        bias = sb("bias", [128, 3 * CT], F32)
        rows = sb("rows", [1, 5 * CH], F32)
        fx = sb("fx", [128, max(8, CT)], F32)
        ones1 = sb("ones1", [1, 128], F32)
        onesk = sb("onesk", [128, 1], F32)
        onesb = sb("onesb", [128, 1], BF16)
        sqb = [sq[b][:].bitcast(BF16) for b in range(2)]

        PS = [st.enter_context(nc.psum_tensor(f"ps{b}", [128, 512], F32))
              for b in range(8)]
        PS_G, PS_U, PS_D0, PS_D1, PS_S1, PS_S2, PS_BA, PS_BB = PS

        class Sem:
            def __init__(self, name):
                self.h = st.enter_context(nc.semaphore(name))
                self.n = 0
                self.name = name

        sPE, sDV, sAC = Sem("sPE"), Sem("sDV"), Sem("sAC")
        sLX, sSX = Sem("sLX"), Sem("sSX")
        sWF = [Sem("sWF0"), Sem("sWF1")]
        sWA = [Sem("sWA0"), Sem("sWA1")]
        sWB = [Sem("sWB0"), Sem("sWB1")]
        sMI = [Sem("sMI0"), Sem("sMI1")]
        sSW = Sem("sSW")
        sAWR = Sem("sAWR")
        sCC = Sem("sCC")
        sLV = Sem("sLV")
        sONE = Sem("sONE")

        last_wait = {}

        def W(eng, ekey, sem, val):
            if val <= 0:
                return
            assert val <= sem.n, (sem.name, val, sem.n)
            if last_wait.get((ekey, sem.name), 0) >= val:
                return
            eng.wait_ge(sem.h, val)
            last_wait[(ekey, sem.name)] = val

        def inc(instr, sem, amt=1):
            instr.then_inc(sem.h, amt)
            sem.n += amt
            return sem.n

        block = st.enter_context(nc.Block())

        bank_free = {id(p): (sDV, 0) for p in PS}
        bank_ready = {}

        slot_free_wf = [(sPE, 0), (sPE, 0)]
        slot_free_wa = [(sPE, 0), (sPE, 0)]
        x_ready = [[(sDV, 0)] * CT for _ in range(NCH)]
        ln_free = (sPE, 0)
        kvr_free = (sDV, 0)
        sg_free = (sDV, 0)
        rwkv_cnt = 0

        @block.gpsimd
        def _(g):
            inc(g.dma_start(out=vt[:], in_=vecs), sLV, 16)
            for ct in range(CT):
                inc(g.dma_start(out=xs[0][:, ct * CH:(ct + 1) * CH],
                                in_=xT[ct, :, 0:CH]), sLX, 16)

        @block.vector
        def _(v):
            W(v, "V", sLV, 16)
            for m, vrow in enumerate((V_TMK, V_TMV, V_TMR)):
                tm = vt[:, vrow * CT:(vrow + 1) * CT]
                xxv = vt[:, V_XX * CT:(V_XX + 1) * CT]
                dst = ckv[:, m * CT:(m + 1) * CT]
                v.tensor_tensor(fx[:, 0:CT], xxv, tm, ALU.mult)
                v.drain()
                v.tensor_tensor(dst, xxv, fx[:, 0:CT], ALU.subtract)
                v.drain()
            inc(v.drain(), sDV, 1)
        ck_ms = sDV.n

        @block.gpsimd
        def _(g):
            g.memset(ones1[:], 1.0)
            g.memset(onesb[:], 1.0)
            inc(g.memset(onesk[:], 1.0), sONE, 1)
        ones_ms = sONE.n

        bias_ms = [0]

        def conv_slab(j):
            m, ot = j // CT, j % CT
            slot = j % 2

            @block.gpsimd
            def _(g):
                fs, fv = slot_free_wa[slot]
                W(g, "G", fs, fv)
                inc(g.dma_start(out=was[slot][:], in_=aw[m][ot]), sWB[slot], 16)

            if m < 3:
                @block.tensor
                def _(t):
                    W(t, "T", sWB[slot], sWB[slot].n)
                    W(t, "T", sDV, ck_ms)
                    for ct in range(CT):
                        mm = t.matmul(PS_D1[:, j:j + 1],
                                      was[slot][:, ct * 128:(ct + 1) * 128],
                                      ckv[:, m * CT + ct:m * CT + ct + 1],
                                      start=(ct == 0), stop=(ct == CT - 1))
                    inc(mm, sPE, 1)

            @block.vector
            def _(v):
                W(v, "V", sLV, 16)
                if m < 3:
                    W(v, "V", sPE, sPE.n)
                else:
                    W(v, "V", sWB[slot], sWB[slot].n)
                W(v, "V", sAWR, 16 * j)
                if m < 3:
                    for ct in range(CT):
                        v.tensor_scalar(wab[1][:, ct * 128:(ct + 1) * 128],
                                        was[slot][:, ct * 128:(ct + 1) * 128],
                                        vt[:, (V_TMK + m) * CT + ct:
                                           (V_TMK + m) * CT + ct + 1],
                                        None, ALU.mult)
                else:
                    v.tensor_copy(wab[1][:], was[slot][:])
                if j == 3 * CT - 1:
                    v.tensor_tensor(bias[:, 0:CT], PS_D1[:, 0:CT],
                                    vt[:, V_TF * CT:(V_TF + 1) * CT], ALU.add)
                    v.tensor_copy(bias[:, CT:3 * CT], PS_D1[:, CT:3 * CT])
                inc(v.drain(), sDV, 1)
                slot_free_wa[slot] = (sDV, sDV.n)
                if j == 3 * CT - 1:
                    bank_free[id(PS_D1)] = (sDV, sDV.n)
            if j == 3 * CT - 1:
                bias_ms[0] = sDV.n
            cms = sDV.n

            @block.sync
            def _(y):
                W(y, "Y", sDV, cms)
                inc(y.dma_start(out=acache[m, ot], in_=wab[1][:]), sAWR, 16)

        def conv_finish():
            @block.sync
            def _(y):
                W(y, "Y", sAWR, sAWR.n)



        mat_dve_done = [0] * max(cfg.SPC, 1)
        a1_hooks = []


        @block.gpsimd
        def _(g):
            for k0 in range(min(2, cfg.SPC)):
                inc(g.dma_start(out=mi[k0][:, 0:C], in_=gidx[k0]), sMI[k0], 16)

        for k in range(cfg.SPC):
            slot = k % 2

            @block.vector
            def _(v, k=k, slot=slot):
                W(v, "V", sMI[slot], 16 * (k // 2 + 1))
                if k >= 2:
                    W(v, "V", sSW, 16 * (k - 1))
                v.tensor_copy(matbf[:], mi[slot][:, 0:C])
                mo = mato[k % 2]
                v.tensor_scalar(mo[:], matbf[:], 0.0, float(lut_vals[0]),
                                ALU.is_equal, ALU.mult)
                for j in range(1, 16):
                    v.tensor_scalar(mattmp[:], matbf[:], float(j),
                                    float(lut_vals[j]), ALU.is_equal, ALU.mult)
                    lastv = v.tensor_tensor(mo[:], mo[:], mattmp[:], ALU.add)
                inc(lastv, sDV, 1)
                mat_dve_done[k] = sDV.n

            @block.gpsimd
            def _(g, k=k, slot=slot):
                W(g, "G", sDV, mat_dve_done[k])
                if k + 2 < cfg.SPC:
                    inc(g.dma_start(out=mi[slot][:, 0:C], in_=gidx[k + 2]),
                        sMI[slot], 16)
                if cfg.use_collective:
                    dst = cc_in[k * 128 * C:(k + 1) * 128 * C].rearrange(
                        "(p f) -> p f", p=128)
                else:
                    dst = cc_out[k * 128 * C:(k + 1) * 128 * C].rearrange(
                        "(p f) -> p f", p=128)
                inc(g.dma_start(out=dst, in_=mato[k % 2][:]), sSW, 16)
                if cfg.use_collective:
                    W(g, "G", sSW, 16 * (k + 1))
                    gsz = cfg.NCORES * 128 * C
                    inc(g.collective_compute(
                        "AllGather", ALU.bypass,
                        replica_groups=[list(range(cfg.NCORES))],
                        ins=[cc_in[k * 128 * C:(k + 1) * 128 * C]
                             .rearrange("(r c) -> r c", c=C).opt()],
                        outs=[cc_out[k * gsz:(k + 1) * gsz]
                              .rearrange("(r c) -> r c", c=C).opt()],
                    ), sCC, 1)
                else:
                    inc(g.nop(), sCC, 1)
            for fn in a1_hooks:
                fn(k)

        @block.gpsimd
        def _(g):
            W(g, "G", sSW, sSW.n)


        def make_stats_hook(ch, xset, same_phase, bf16_ok=True):
            st_state = {"pe": [0, 0], "ms": 0}

            def hook(ct, xr=None):
                b = ct % 2
                xrs, xrv = xr if (same_phase and xr is not None) \
                    else x_ready[ch][ct]

                @block.scalar
                def _(a, ct=ct, b=b, xrs=xrs, xrv=xrv):
                    W(a, "A", xrs, xrv)
                    if ct == 0:
                        W(a, "A", sLX, 16 * CT * (ch + 1))
                    if ct >= 2:
                        W(a, "A", sPE, st_state["pe"][b])
                    if bf16_ok:
                        a.activation(lnT[:, ct * CH:(ct + 1) * CH],
                                     xs[xset][:, ct * CH:(ct + 1) * CH],
                                     AF.Identity)
                        inc(a.activation(sqb[b][:, 0:CH],
                                         xs[xset][:, ct * CH:(ct + 1) * CH],
                                         AF.Square), sAC, 1)
                    else:
                        inc(a.activation(sq[b][:, 0:CH],
                                         xs[xset][:, ct * CH:(ct + 1) * CH],
                                         AF.Square), sAC, 1)

                @block.tensor
                def _(t, ct=ct, b=b, xrs=xrs, xrv=xrv):
                    W(t, "T", sAC, sAC.n)
                    W(t, "T", xrs, xrv)
                    if ct == 0:
                        W(t, "T", sLX, 16 * CT * (ch + 1))
                        W(t, "T", sONE, ones_ms)
                        for p in (PS_S1, PS_S2):
                            bs, bv = bank_free[id(p)]
                            W(t, "T", bs, bv)
                    if bf16_ok:
                        t.matmul(PS_S1[0:1, 0:CH], onesb[:],
                                 lnT[:, ct * CH:(ct + 1) * CH],
                                 start=(ct == 0), stop=(ct == CT - 1))
                        mm = t.matmul(PS_S2[0:1, 0:CH], onesb[:],
                                      sqb[b][:, 0:CH],
                                      start=(ct == 0), stop=(ct == CT - 1))
                    else:
                        t.matmul(PS_S1[0:1, 0:CH], onesk[:],
                                 xs[xset][:, ct * CH:(ct + 1) * CH],
                                 start=(ct == 0), stop=(ct == CT - 1))
                        mm = t.matmul(PS_S2[0:1, 0:CH], onesk[:], sq[b][:, 0:CH],
                                      start=(ct == 0), stop=(ct == CT - 1))
                    inc(mm, sPE, 1)
                    st_state["pe"][b] = sPE.n
                    if ct == CT - 1:
                        st_state["ms"] = sPE.n
            return hook, st_state

        def emit_ln(ch, xset, tag, stats_done=None):
            nonlocal ln_free
            sl = slice(0, CH)

            if stats_done is None:
                hook, st_state = make_stats_hook(ch, xset, same_phase=False,
                                                 bf16_ok=False)
                for ct in range(CT):
                    hook(ct)
                stats_ms = st_state["ms"]
            else:
                stats_ms = stats_done
            bank_ready[id(PS_S1)] = (sPE, stats_ms)
            bank_ready[id(PS_S2)] = (sPE, stats_ms)

            m_row = rows[0:1, 0:CH]
            v_row = rows[0:1, CH:2 * CH]
            a_row = rows[0:1, 2 * CH:3 * CH]
            b_row = rows[0:1, 3 * CH:4 * CH]
            t_row = rows[0:1, 4 * CH:5 * CH]
            t2_row = v_row

            @block.vector
            def _(v):
                W(v, "V", sPE, stats_ms)
                v.tensor_scalar(m_row, PS_S1[0:1, sl], 1.0 / C, None, ALU.mult)
                v.tensor_scalar(v_row, PS_S2[0:1, sl], 1.0 / C, 1e-5,
                                ALU.mult, ALU.add)
                v.drain()
                v.tensor_tensor(t_row, m_row, m_row, ALU.mult)
                v.drain()
                v.tensor_tensor(v_row, v_row, t_row, ALU.subtract)
                inc(v.drain(), sDV, 1)
                bank_free[id(PS_S1)] = (sDV, sDV.n)
                bank_free[id(PS_S2)] = (sDV, sDV.n)
                EMIT_LOG.append(("rows_read", ch, tag, stats_ms, sDV.n))

            @block.scalar
            def _(a):
                W(a, "A", sDV, sDV.n)
                a.activation(t_row, v_row, AF.Ln)
                inc(a.drain(), sAC, 1)

            @block.vector
            def _(v):
                W(v, "V", sAC, sAC.n)
                v.tensor_scalar(t2_row, t_row, -0.5, None, ALU.mult)
                inc(v.drain(), sDV, 1)

            @block.scalar
            def _(a):
                W(a, "A", sDV, sDV.n)
                a.activation(a_row, t2_row, AF.Exp)
                inc(a.drain(), sAC, 1)

            @block.vector
            def _(v):
                W(v, "V", sAC, sAC.n)
                v.tensor_tensor(t_row, m_row, a_row, ALU.mult)
                v.drain()
                v.tensor_scalar(b_row, t_row, -1.0, None, ALU.mult)
                inc(v.drain(), sDV, 1)
            rows_ms = sDV.n

            @block.tensor
            def _(t):
                W(t, "T", sDV, rows_ms)
                for p in (PS_BA, PS_BB):
                    bs, bv = bank_free[id(p)]
                    W(t, "T", bs, bv)
                t.matmul(PS_BA[:, sl], ones1[:], a_row, start=True, stop=True)
                mm = t.matmul(PS_BB[:, sl], ones1[:], b_row, start=True, stop=True)
                inc(mm, sPE, 1)
                bank_ready[id(PS_BA)] = (sPE, sPE.n)
                bank_ready[id(PS_BB)] = (sPE, sPE.n)
            bc_ms = sPE.n

            ln_tiles = [0] * CT

            @block.vector
            def _(v):
                W(v, "V", sPE, bc_ms)
                fs, fv = ln_free
                W(v, "V", fs, fv)
                for ct in range(CT):
                    xap = xs[xset][:, ct * CH:(ct + 1) * CH]
                    v.tensor_tensor(tmpA[:, sl], xap, PS_BA[:, sl], ALU.mult)
                    inc(v.tensor_tensor(lnT[:, ct * CH:(ct + 1) * CH],
                                        tmpA[:, sl], PS_BB[:, sl], ALU.add),
                        sDV, 1)
                    ln_tiles[ct] = sDV.n
                bank_free[id(PS_BA)] = (sDV, sDV.n)
                bank_free[id(PS_BB)] = (sDV, sDV.n)
            return ln_tiles

        fs_cnt = [0]

        def ffn_slab_load(src_slabs, slot):
            if cfg.use_collective:
                need = max(s // cfg.NCORES for s, _ in src_slabs) + 1
            else:
                need = max(s for s, _ in src_slabs) + 1

            @block.sync
            def _(y):
                W(y, "Y", sCC, need)
                fsm, fvv = slot_free_wf[slot]
                W(y, "Y", fsm, fvv)
                for s, coff in src_slabs:
                    w = min(C, I - coff)
                    inc(y.dma_start(out=wfs[slot][:, coff:coff + w],
                                    in_=slab_ap(s)[:, 0:w]), sWF[slot], 16)

        conv_queue = list(range(4 * CT))
        assert 3 * CT <= 2 * IT

        def emit_ffn(ch, xset, ln_ms, conv_hook=None, tail_hook=None):
            nonlocal sg_free
            sl = slice(0, CH)
            h_ms = 0
            silu_ms = {}
            for it in range(IT):
                if conv_hook is not None and conv_queue:
                    for _ in range(2):
                        if conv_queue:
                            conv_hook(conv_queue.pop(0))
                for half, base in ((0, 0), (1, IT)):
                    slot = fs_cnt[0] % 2
                    fs_cnt[0] += 1
                    ffn_slab_load([(base + it, 0)], slot)

                    @block.tensor
                    def _(t, it=it, half=half, slot=slot):
                        W(t, "T", sWF[slot], sWF[slot].n)
                        p = PS_G if half == 0 else PS_U
                        bs, bv = bank_free[id(p)]
                        W(t, "T", bs, bv)
                        for ht in range(CT):
                            W(t, "T", sDV, ln_ms[ht])
                            mm = t.matmul(p[:, sl],
                                          wfs[slot][:, ht * 128:(ht + 1) * 128],
                                          lnT[:, ht * CH:(ht + 1) * CH],
                                          start=(ht == 0), stop=(ht == CT - 1))
                        inc(mm, sPE, 1)
                        bank_ready[id(p)] = (sPE, sPE.n)
                        slot_free_wf[slot] = (sPE, sPE.n)

                @block.scalar
                def _(a, it=it):
                    rs, rv = bank_ready[id(PS_G)]
                    W(a, "A", rs, rv)
                    fs2, fv2 = sg_free
                    W(a, "A", fs2, fv2)
                    inc(a.activation(sg[:, sl], PS_G[:, sl], AF.Silu), sAC, 1)
                    silu_ms[it] = sAC.n
                    bank_free[id(PS_G)] = (sAC, sAC.n)

                @block.vector
                def _(v, it=it):
                    W(v, "V", sAC, silu_ms[it])
                    rs, rv = bank_ready[id(PS_U)]
                    W(v, "V", rs, rv)
                    inc(v.tensor_tensor(hT[:, it * CH:(it + 1) * CH], sg[:, sl],
                                        PS_U[:, sl], ALU.mult), sDV, 1)
                    bank_free[id(PS_U)] = (sDV, sDV.n)
                    sg_free = (sDV, sDV.n)
                h_ms = sDV.n

            res_list = [None] * CT
            for ot in range(CT):
                if ot >= 2 and tail_hook is not None:
                    tail_hook(ot - 2, res_list[ot - 2])
                slot = fs_cnt[0] % 2
                fs_cnt[0] += 1
                ffn_slab_load([(2 * IT + ot * DS + j, j * C) for j in range(DS)],
                              slot)
                p = PS_D0 if ot % 2 == 0 else PS_D1

                @block.tensor
                def _(t, ot=ot, slot=slot, p=p):
                    W(t, "T", sWF[slot], sWF[slot].n)
                    W(t, "T", sDV, h_ms)
                    bs, bv = bank_free[id(p)]
                    W(t, "T", bs, bv)
                    for it in range(IT):
                        mm = t.matmul(p[:, sl],
                                      wfs[slot][:, it * 128:(it + 1) * 128],
                                      hT[:, it * CH:(it + 1) * CH],
                                      start=(it == 0), stop=(it == IT - 1))
                    inc(mm, sPE, 1)
                    bank_ready[id(p)] = (sPE, sPE.n)
                    slot_free_wf[slot] = (sPE, sPE.n)

                @block.vector
                def _(v, ot=ot, p=p):
                    rs, rv = bank_ready[id(p)]
                    W(v, "V", rs, rv)
                    inc(v.scalar_tensor_tensor(
                        xs[xset][:, ot * CH:(ot + 1) * CH], p[:, sl], 0.5,
                        xs[xset][:, ot * CH:(ot + 1) * CH], ALU.mult, ALU.add),
                        sDV, 1)
                    bank_free[id(p)] = (sDV, sDV.n)
                res_list[ot] = (sDV, sDV.n)
            if tail_hook is not None:
                for ct in range(CT - 2, CT):
                    tail_hook(ct, res_list[ct])
            return res_list

        a_cnt = [4 * CT]

        def attn_slab_load(m, ot, slot):
            @block.sync
            def _(y):
                fsm, fvv = slot_free_wa[slot]
                W(y, "Y", fsm, fvv)
                inc(y.dma_start(out=wab[slot][:], in_=acache[m, ot]), sWA[slot], 16)

        def emit_attn(ch, xset, ln_ms, tail_hook=None):
            nonlocal kvr_free, rwkv_cnt, ln_free
            sl = slice(0, CH)
            chain_ms = 0
            for ot in range(CT):
                psum_of = {0: PS_G, 1: PS_U, 2: PS_D0}
                for m in range(3):
                    slot = a_cnt[0] % 2
                    a_cnt[0] += 1
                    attn_slab_load(m, ot, slot)
                    p = psum_of[m]

                    @block.tensor
                    def _(t, m=m, ot=ot, slot=slot, p=p):
                        W(t, "T", sWA[slot], sWA[slot].n)
                        bs, bv = bank_free[id(p)]
                        W(t, "T", bs, bv)
                        for ct in range(CT):
                            W(t, "T", sDV, ln_ms[ct])
                            mm = t.matmul(p[:, sl],
                                          wab[slot][:, ct * 128:(ct + 1) * 128],
                                          lnT[:, ct * CH:(ct + 1) * CH],
                                          start=(ct == 0), stop=(ct == CT - 1))
                        inc(mm, sPE, 1)
                        bank_ready[id(p)] = (sPE, sPE.n)
                        slot_free_wa[slot] = (sPE, sPE.n)

                @block.scalar
                def _(a, ot=ot):
                    W(a, "A", sDV, bias_ms[0])
                    fs2, fv2 = kvr_free
                    W(a, "A", fs2, fv2)
                    for m, (p, dst) in enumerate(((PS_G, k_s), (PS_U, v_s),
                                                  (PS_D0, r_s))):
                        rs, rv = bank_ready[id(p)]
                        W(a, "A", rs, rv)
                        fn = AF.Sigmoid if m == 2 else AF.Identity
                        lasta = a.activation(
                            dst[:, sl], p[:, sl], fn,
                            bias=bias[:, m * CT + ot:m * CT + ot + 1],
                            scale=1.0)
                    inc(lasta, sAC, 1)
                    bank_free[id(PS_G)] = (sAC, sAC.n)
                    bank_free[id(PS_U)] = (sAC, sAC.n)
                    bank_free[id(PS_D0)] = (sAC, sAC.n)
                evac_ms = sAC.n

                @block.vector
                def _(v, ot=ot):
                    W(v, "V", sAC, evac_ms)
                    v.tensor_scalar(tmpA[:, sl], k_s[:, sl], 0.0, None, ALU.min)
                    lastv = v.tensor_scalar(tmpC[:, sl], k_s[:, sl], -1.0, 0.0,
                                            ALU.mult, ALU.max)
                    inc(lastv, sDV, 1)
                m2_ms = sDV.n

                @block.scalar
                def _(a, ot=ot):
                    W(a, "A", sDV, m2_ms)
                    a.activation(tmpB[:, sl], tmpA[:, sl], AF.Exp)
                    lasta = a.activation(tmpD[:, sl], tmpC[:, sl], AF.Exp)
                    inc(lasta, sAC, 1)
                exp_ms = sAC.n
                if debug and ch == 0 and ot == 0:
                    @block.gpsimd
                    def _(g):
                        W(g, "G", sAC, evac_ms)
                        inc(g.dma_start(out=dbg["d_kvr"][:, 0:CH],
                                        in_=k_s[:, sl]), sSX, 16)
                        inc(g.dma_start(out=dbg["d_kvr"][:, CH:2 * CH],
                                        in_=v_s[:, sl]), sSX, 16)
                        inc(g.dma_start(out=dbg["d_kvr"][:, 2 * CH:3 * CH],
                                        in_=r_s[:, sl]), sSX, 16)

                @block.vector
                def _(v, ot=ot):
                    W(v, "V", sAC, exp_ms)
                    v.tensor_tensor(tmpB[:, sl], tmpB[:, sl], v_s[:, sl], ALU.mult)
                    v.tensor_tensor(tmpB[:, sl], tmpB[:, sl], tmpD[:, sl], ALU.mult)
                    inc(v.tensor_tensor(rwkv[:, ot * CH:(ot + 1) * CH],
                                        tmpB[:, sl], r_s[:, sl], ALU.mult),
                        sDV, 1)

                if ch == 0:
                    f1 = fx[:, 0:1]; f2 = fx[:, 1:2]; f3 = fx[:, 2:3]
                    f4 = fx[:, 3:4]; f5 = fx[:, 4:5]; f6 = fx[:, 5:6]
                    f7 = fx[:, 6:7]; f8 = fx[:, 7:8]
                    ppc = vt[:, V_PP * CT + ot:V_PP * CT + ot + 1]
                    aac = vt[:, V_AA * CT + ot:V_AA * CT + ot + 1]
                    bbc = vt[:, V_BB * CT + ot:V_BB * CT + ot + 1]

                    @block.vector
                    def _(v, ot=ot):
                        v.tensor_tensor(f1, ppc, k_s[:, 0:1], ALU.max)
                        v.drain()
                        v.tensor_tensor(f2, ppc, f1, ALU.subtract)
                        v.tensor_tensor(f3, k_s[:, 0:1], f1, ALU.subtract)
                        inc(v.drain(), sDV, 1)

                    @block.scalar
                    def _(a, ot=ot):
                        W(a, "A", sDV, sDV.n)
                        a.activation(f6, f2, AF.Exp)
                        a.activation(f7, f3, AF.Exp)
                        inc(a.drain(), sAC, 1)

                    @block.vector
                    def _(v, ot=ot):
                        W(v, "V", sAC, sAC.n)
                        v.tensor_tensor(f4, f6, aac, ALU.mult)
                        v.tensor_tensor(f5, f7, v_s[:, 0:1], ALU.mult)
                        v.drain()
                        v.tensor_tensor(f4, f4, f5, ALU.add)
                        v.tensor_tensor(f2, f6, bbc, ALU.mult)
                        v.drain()
                        v.tensor_tensor(f2, f2, f7, ALU.add)
                        v.drain()
                        v.reciprocal(f8, f2)
                        v.drain()
                        v.tensor_tensor(f4, f4, f8, ALU.mult)
                        v.drain()
                        v.tensor_tensor(rwkv[:, ot * CH:ot * CH + 1], f4,
                                        r_s[:, 0:1], ALU.mult)
                        inc(v.drain(), sDV, 1)

                chain_ms = sDV.n
                kvr_free = (sDV, chain_ms)

            res_list = [None] * CT
            for ot in range(CT):
                if ot >= 2 and tail_hook is not None:
                    tail_hook(ot - 2, res_list[ot - 2])
                slot = a_cnt[0] % 2
                a_cnt[0] += 1
                attn_slab_load(3, ot, slot)
                p = PS_D1 if ot % 2 == 0 else PS_D0

                @block.tensor
                def _(t, ot=ot, slot=slot, p=p):
                    W(t, "T", sWA[slot], sWA[slot].n)
                    W(t, "T", sDV, chain_ms)
                    bs, bv = bank_free[id(p)]
                    W(t, "T", bs, bv)
                    for ct in range(CT):
                        mm = t.matmul(p[:, sl],
                                      wab[slot][:, ct * 128:(ct + 1) * 128],
                                      rwkv[:, ct * CH:(ct + 1) * CH],
                                      start=(ct == 0), stop=(ct == CT - 1))
                    inc(mm, sPE, 1)
                    bank_ready[id(p)] = (sPE, sPE.n)
                    slot_free_wa[slot] = (sPE, sPE.n)
                if ot == CT - 1:
                    ln_free = (sPE, sPE.n)

                @block.vector
                def _(v, ot=ot, p=p):
                    rs, rv = bank_ready[id(p)]
                    W(v, "V", rs, rv)
                    inc(v.scalar_tensor_tensor(
                        xs[xset][:, ot * CH:(ot + 1) * CH], p[:, sl], 1.0,
                        xs[xset][:, ot * CH:(ot + 1) * CH], ALU.mult, ALU.add),
                        sDV, 1)
                    bank_free[id(p)] = (sDV, sDV.n)
                res_list[ot] = (sDV, sDV.n)
            if tail_hook is not None:
                for ct in range(CT - 2, CT):
                    tail_hook(ct, res_list[ct])
            return res_list

        def dump(name, src_ap, width, wait_sem, wait_val):
            if not debug:
                return
            @block.gpsimd
            def _(g):
                W(g, "G", wait_sem, wait_val)
                inc(g.dma_start(out=dbg[name], in_=src_ap), sSX, 16)

        def dump_dram(name, src_ap, width, wait_sem, wait_val):
            if not debug:
                return
            @block.gpsimd
            def _(g):
                W(g, "G", wait_sem, wait_val)
                inc(g.dma_start(out=dbg[name], in_=src_ap), sSX, 16)

        if debug:
            dump_dram("d_slab", slab_ap(0), C, sCC, 1)
            dump("d_ckv", ckv[:], 3 * CT, sDV, ck_ms)
            dump("d_vt", vt[:], 8 * CT, sLV, 16)

        store_ms = [0] * (NCH + 2)

        def emit_xload(ch):
            if ch == 0 or ch >= NCH:
                return

            @block.gpsimd
            def _(g, ch=ch):
                if ch >= 2:
                    W(g, "G", sSX, store_ms[ch - 2])
                for ct in range(CT):
                    inc(g.dma_start(
                        out=xs[ch % 2][:, ct * CH:(ct + 1) * CH],
                        in_=xT[ct, :, ch * CH:(ch + 1) * CH]), sLX, 16)

        def emit_ffn1(ch, stats_done=None, tail_hook=None):
            xset = ch % 2
            ln1 = emit_ln(ch, xset, "ffn1", stats_done=stats_done)
            if ch == 0:
                dump("d_ln1", lnT[:], CT * CH, sDV, ln1[-1])
            r1 = emit_ffn(ch, xset, ln1,
                          conv_hook=(conv_slab if ch <= 1 else None),
                          tail_hook=tail_hook)
            x_ready[ch] = r1
            if ch == 0:
                dump("d_h", hT[:], IT * CH, sDV, r1[-1][1])
                dump("d_x1", xs[xset][:], CT * CH, sDV, r1[-1][1])

        def emit_rest(ch, pre_attn_stats=None, next_hook=None, use_hook2=True):
            xset = ch % 2
            lna = emit_ln(ch, xset, "attn", stats_done=pre_attn_stats)
            if use_hook2:
                hook2, st2 = make_stats_hook(ch, xset, same_phase=True)
            r2 = emit_attn(ch, xset, lna,
                           tail_hook=(hook2 if use_hook2 else None))
            x_ready[ch] = r2
            if ch == 0:
                dump("d_x2", xs[xset][:], CT * CH, sDV, r2[-1][1])
            ln2 = emit_ln(ch, xset, "ffn2",
                          stats_done=(st2["ms"] if use_hook2 else None))
            if ch == 0:
                dump("d_ln3", lnT[:], CT * CH, sDV, ln2[-1])
            r3 = emit_ffn(ch, xset, ln2, tail_hook=next_hook)
            x_ready[ch] = r3

            @block.gpsimd
            def _(g, ch=ch, xset=xset, r3=r3):
                W(g, "G", sDV, r3[-1][1])
                for ct in range(CT):
                    inc(g.dma_start(out=outT[ct, :, ch * CH:(ch + 1) * CH],
                                    in_=xs[xset][:, ct * CH:(ct + 1) * CH]),
                        sSX, 16)
            store_ms[ch] = sSX.n

        emit_xload(0)
        emit_xload(1)
        HOOKS = True
        HOOK2 = True
        emit_ffn1(0)
        hA0, stA0 = make_stats_hook(0, 0, same_phase=False)
        emit_ffn1(1, tail_hook=(hA0 if HOOKS else None))
        conv_finish()
        if debug:
            dump_dram("d_aw", acache[0, 0], C, sAWR, 16 * 1)
            dump("d_bias", bias[:], 3 * CT, sDV, bias_ms[0])
        hooks = {}
        if NCH > 1:
            hooks[1], hooks["s1"] = make_stats_hook(1, 1, same_phase=False)
        if NCH > 2:
            hooks[2], hooks["s2"] = make_stats_hook(2, 0, same_phase=False)
        if NCH > 3:
            hooks[3], hooks["s3"] = make_stats_hook(3, 1, same_phase=False)
        emit_rest(0, use_hook2=HOOK2, pre_attn_stats=(stA0["ms"] if HOOKS else None),
                  next_hook=(hooks.get(1) if HOOKS else None))
        emit_xload(2)
        emit_rest(1, use_hook2=HOOK2,
                  pre_attn_stats=(hooks["s1"]["ms"] if HOOKS and NCH > 1 else None),
                  next_hook=(hooks.get(2) if HOOKS else None))
        emit_xload(3)
        if NCH > 2:
            emit_ffn1(2, stats_done=(hooks["s2"]["ms"] if HOOKS else None))
            emit_rest(2, use_hook2=HOOK2, pre_attn_stats=None,
                      next_hook=(hooks.get(3) if HOOKS else None))
        if NCH > 3:
            emit_ffn1(3, stats_done=(hooks["s3"]["ms"] if HOOKS else None))
            emit_rest(3, use_hook2=HOOK2, pre_attn_stats=None, next_hook=None)

        @block.gpsimd
        def _(g):
            W(g, "G", sSX, store_ms[NCH - 1])

    return nc



def _wstream(w, CT):
    wT = np.ascontiguousarray(np.asarray(w, dtype=np.float32).T)
    C = wT.shape[0]
    s = wT.reshape(CT, 128, CT, 128).transpose(2, 1, 0, 3)
    return np.ascontiguousarray(s).reshape(CT, 128, C)


def _ghost_stream(gate_idx, up_idx, down_idx, cfg):
    C, I, CT, IT, DS = cfg.C, cfg.I, cfg.CT, cfg.IT, cfg.DS
    gT = np.ascontiguousarray(np.asarray(gate_idx, dtype=np.int32).T)
    uT = np.ascontiguousarray(np.asarray(up_idx, dtype=np.int32).T)
    dT = np.ascontiguousarray(np.asarray(down_idx, dtype=np.int32).T)
    g = gT.reshape(CT, 128, IT, 128).transpose(2, 1, 0, 3).reshape(IT, 128, C)
    u = uT.reshape(CT, 128, IT, 128).transpose(2, 1, 0, 3).reshape(IT, 128, C)
    d = dT.reshape(IT, 128, CT, 128).transpose(2, 1, 0, 3).reshape(CT, 128, I)
    dpad = np.zeros((CT, 128, cfg.IPAD), dtype=np.int32)
    dpad[:, :, :I] = d
    d_sl = dpad.reshape(CT, 128, DS, C).transpose(0, 2, 1, 3)
    flat = np.concatenate([g.reshape(-1), u.reshape(-1), d_sl.reshape(-1)])
    pad = np.zeros(cfg.S_PAD * 128 * C, dtype=np.int32)
    pad[:flat.size] = flat
    return pad.reshape(cfg.S_PAD, 128, C)


def _core_slabs(ghost, cfg, r):
    if not cfg.use_collective:
        return ghost
    ks = [cfg.NCORES * k + r for k in range(cfg.SPC)]
    return np.ascontiguousarray(ghost[ks])


def _vecs_for_core(tf, tmk, tmv, tmr, xx, aa, bb, pp, cfg, seq_start):
    C, CT = cfg.C, cfg.CT
    z = np.zeros(C, dtype=np.float32)
    neg = np.full(C, -1e38, dtype=np.float32)
    rows = [np.asarray(v, dtype=np.float32).reshape(C) for v in
            (tf, tmk, tmv, tmr, xx)]
    rows += [np.asarray(aa, np.float32).reshape(C) if seq_start else z,
             np.asarray(bb, np.float32).reshape(C) if seq_start else z,
             np.asarray(pp, np.float32).reshape(C) if seq_start else neg]
    v2 = np.stack(rows)
    return np.ascontiguousarray(
        v2.reshape(8, CT, 128).transpose(2, 0, 1).reshape(128, 8 * CT))


def run(cfg, inputs):
    x = np.asarray(inputs["L0_attn_x"], dtype=np.float32)
    B, T, C = x.shape
    xf = x.reshape(B * T, C)
    lut = [float(v) for v in np.asarray(inputs["L0_ffn_lut"], dtype=np.float32)]

    ghost = _ghost_stream(inputs["L0_ffn_gate_idx"], inputs["L0_ffn_up_idx"],
                          inputs["L0_ffn_down_idx"], cfg)
    aw_streams = [_wstream(inputs[k], cfg.CT) for k in
                  ("L0_attn_w_key", "L0_attn_w_value", "L0_attn_w_rec",
                   "L0_attn_w_out")]

    in_maps = []
    for r in range(cfg.NCORES):
        xc = xf[r * cfg.TCORE:(r + 1) * cfg.TCORE]
        xTc = np.ascontiguousarray(xc.T).reshape(cfg.CT, 128, cfg.TCORE)
        seq_start = (r * cfg.TCORE) % cfg.SEQ == 0
        im = {
            "xT": xTc,
            "vecs": _vecs_for_core(
                inputs["L0_attn_time_first"], inputs["L0_attn_time_mix_k"],
                inputs["L0_attn_time_mix_v"], inputs["L0_attn_time_mix_r"],
                inputs["L0_attn_xx"], inputs["L0_attn_aa"],
                inputs["L0_attn_bb"], inputs["L0_attn_pp"], cfg, seq_start),
            "gidx": _core_slabs(ghost, cfg, r),
        }
        for m in range(4):
            im[f"aw{m}"] = aw_streams[m]
        in_maps.append(im)

    nc = build_program(cfg, lut)
    res = run_bass_kernel_spmd(nc, in_maps, core_ids=list(range(cfg.NCORES)))

    outs = []
    for r in range(cfg.NCORES):
        o = np.asarray(res.results[r]["outT"], dtype=np.float32)
        outs.append(np.ascontiguousarray(o.reshape(C, cfg.TCORE).T))
    return np.concatenate(outs, axis=0).reshape(B, T, C)


def kernel(**inputs):
    return run(CFG_FULL, inputs)


# version 20
# speedup vs baseline: 1.0239x; 1.0046x over previous
import contextlib
import numpy as np
import ml_dtypes

import concourse.bass as bass
import concourse.mybir as mybir
from concourse.bass_utils import run_bass_kernel_spmd

EMIT_LOG = []
F32 = mybir.dt.float32
BF16 = mybir.dt.bfloat16
I32 = mybir.dt.int32
AF = mybir.ActivationFunctionType
ALU = mybir.AluOpType
BF = ml_dtypes.bfloat16


class Cfg:
    def __init__(self, C, I, TCORE, CHUNK, SEQ, NCORES, use_collective=True):
        self.C = C
        self.I = I
        self.TCORE = TCORE
        self.CHUNK = CHUNK
        self.SEQ = SEQ
        self.NCORES = NCORES
        self.CT = C // 128
        self.IT = I // 128
        self.NCH = TCORE // CHUNK
        self.DS = -(-I // C)
        self.IPAD = self.DS * C
        self.S_REAL = 2 * self.IT + self.CT * self.DS
        self.SPC = -(-self.S_REAL // NCORES)
        self.S_PAD = self.SPC * NCORES
        self.use_collective = use_collective
        if not use_collective:
            self.SPC = self.S_PAD


CFG_FULL = Cfg(C=2048, I=5504, TCORE=2048, CHUNK=512, SEQ=4096, NCORES=8)

V_TF, V_TMK, V_TMV, V_TMR, V_XX, V_AA, V_BB, V_PP = range(8)


def build_program(cfg, lut_vals, debug=False):
    C, I, CT, IT = cfg.C, cfg.I, cfg.CT, cfg.IT
    CH, NCH = cfg.CHUNK, cfg.NCH
    DS = cfg.DS
    nc = bass.Bass("TRN2", target_bir_lowering=False, debug=False)

    xT = nc.dram_tensor("xT", [CT, 128, cfg.TCORE], F32, kind="ExternalInput").ap()
    vecs = nc.dram_tensor("vecs", [128, 8 * CT], F32, kind="ExternalInput").ap()
    gidx = nc.dram_tensor("gidx", [cfg.SPC, 128, C], I32, kind="ExternalInput").ap()
    aw = [nc.dram_tensor(f"aw{m}", [CT, 128, C], F32, kind="ExternalInput").ap()
          for m in range(4)]
    outT = nc.dram_tensor("outT", [CT, 128, cfg.TCORE], F32, kind="ExternalOutput").ap()
    dbg = {}
    if debug:
        for nm, w, dt_ in (("d_ln1", CT * CH, BF16), ("d_h", IT * CH, BF16),
                           ("d_x1", CT * CH, F32), ("d_kvr", 3 * CH, F32),
                           ("d_x2", CT * CH, F32), ("d_slab", C, BF16),
                           ("d_aw", C, BF16), ("d_rwkv", CT * CH, BF16),
                           ("d_bias", 3 * CT, F32), ("d_ckv", 3 * CT, F32),
                           ("d_vt", 8 * CT, F32), ("d_ln3", CT * CH, BF16)):
            dbg[nm] = nc.dram_tensor(nm, [128, w], dt_, kind="ExternalOutput").ap()

    GE_PAD = cfg.S_PAD * 128 * C
    if cfg.use_collective:
        cc_in = nc.dram_tensor("cc_in", [cfg.SPC * 128 * C], BF16, kind="Internal").ap()
    cc_out = nc.dram_tensor("cc_out", [GE_PAD], BF16, kind="Internal").ap()
    acache = nc.dram_tensor("acache", [4, CT, 128, C], BF16, kind="Internal").ap()

    def slab_dst_off(s):
        return s * 128 * C

    def slab_ap(s):
        off = slab_dst_off(s)
        return cc_out[off:off + 128 * C].rearrange("(p f) -> p f", p=128)

    with contextlib.ExitStack() as st:
        def sb(name, shape, dt):
            return st.enter_context(nc.sbuf_tensor(name, shape, dt))

        xs = [sb(f"xs{b}", [128, CT * CH], F32) for b in range(2)]
        lnT = sb("lnT", [128, CT * CH], BF16)
        hT = sb("hT", [128, IT * CH], BF16)
        assert IT >= CT
        rwkv = hT[:, 0:CT * CH]
        wfs = [sb(f"wfs{b}", [128, max(I, 2 * C)], BF16) for b in range(2)]
        if CT * CH * 2 >= 2 * 2 * C:
            mi = [xs[1][:, b * C:(b + 1) * C].bitcast(I32) for b in range(2)]
        else:
            mi = [sb(f"mi{b}", [128, C], I32)[:] for b in range(2)]
        was = [sb(f"was{b}", [128, C], F32) for b in range(2)]
        wab = [sb(f"wab{b}", [128, C], BF16) for b in range(2)]
        mattmp = sb("mattmp", [128, C], BF16)
        mato = [sb(f"mato{b}", [128, C], BF16) for b in range(2)]
        assert C == 4 * CH
        kv_arena = sb("kv_arena", [128, 2 * CH], F32)
        k_s = kv_arena[:, 0:CH]
        v_s = kv_arena[:, CH:2 * CH]
        matbf = kv_arena[:].bitcast(BF16)
        r_s = sb("r_s", [128, CH], F32)
        tmpA = sb("tmpA", [128, CH], F32)
        tmpC = sb("tmpC", [128, CH], F32)
        sq = [sb(f"sq{b}", [128, CH], F32) for b in range(2)]
        tmpB = sq[1]
        tmpD = sq[0]
        sg = sb("sg", [128, CH], BF16)
        vt = sb("vt", [128, 8 * CT], F32)
        ckv = sb("ckv", [128, 3 * CT], F32)
        bias = sb("bias", [128, 3 * CT], F32)
        rows = sb("rows", [1, 5 * CH], F32)
        fx = sb("fx", [128, max(8, CT)], F32)
        ones1 = sb("ones1", [1, 128], F32)
        onesk = sb("onesk", [128, 1], F32)
        onesb = sb("onesb", [128, 1], BF16)
        sqb = [sq[b][:].bitcast(BF16) for b in range(2)]

        PS = [st.enter_context(nc.psum_tensor(f"ps{b}", [128, 512], F32))
              for b in range(8)]
        PS_G, PS_U, PS_D0, PS_D1, PS_S1, PS_S2, PS_BA, PS_BB = PS

        class Sem:
            def __init__(self, name):
                self.h = st.enter_context(nc.semaphore(name))
                self.n = 0
                self.name = name

        sPE, sDV, sAC = Sem("sPE"), Sem("sDV"), Sem("sAC")
        sLX, sSX = Sem("sLX"), Sem("sSX")
        sWF = [Sem("sWF0"), Sem("sWF1")]
        sWA = [Sem("sWA0"), Sem("sWA1")]
        sWB = [Sem("sWB0"), Sem("sWB1")]
        sMI = [Sem("sMI0"), Sem("sMI1")]
        sSW = Sem("sSW")
        sAWR = Sem("sAWR")
        sCC = Sem("sCC")
        sLV = Sem("sLV")
        sONE = Sem("sONE")

        last_wait = {}

        def W(eng, ekey, sem, val):
            if val <= 0:
                return
            assert val <= sem.n, (sem.name, val, sem.n)
            if last_wait.get((ekey, sem.name), 0) >= val:
                return
            eng.wait_ge(sem.h, val)
            last_wait[(ekey, sem.name)] = val

        def inc(instr, sem, amt=1):
            instr.then_inc(sem.h, amt)
            sem.n += amt
            return sem.n

        block = st.enter_context(nc.Block())

        bank_free = {id(p): (sDV, 0) for p in PS}
        bank_ready = {}

        slot_free_wf = [(sPE, 0), (sPE, 0)]
        slot_free_wa = [(sPE, 0), (sPE, 0)]
        x_ready = [[(sDV, 0)] * CT for _ in range(NCH)]
        ln_free = (sPE, 0)
        kvr_free = (sDV, 0)
        sg_free = (sDV, 0)
        rwkv_cnt = 0

        @block.gpsimd
        def _(g):
            inc(g.dma_start(out=vt[:], in_=vecs), sLV, 16)
            for ct in range(CT):
                inc(g.dma_start(out=xs[0][:, ct * CH:(ct + 1) * CH],
                                in_=xT[ct, :, 0:CH]), sLX, 16)

        @block.vector
        def _(v):
            W(v, "V", sLV, 16)
            for m, vrow in enumerate((V_TMK, V_TMV, V_TMR)):
                tm = vt[:, vrow * CT:(vrow + 1) * CT]
                xxv = vt[:, V_XX * CT:(V_XX + 1) * CT]
                dst = ckv[:, m * CT:(m + 1) * CT]
                v.tensor_tensor(fx[:, 0:CT], xxv, tm, ALU.mult)
                v.drain()
                v.tensor_tensor(dst, xxv, fx[:, 0:CT], ALU.subtract)
                v.drain()
            inc(v.drain(), sDV, 1)
        ck_ms = sDV.n

        @block.gpsimd
        def _(g):
            g.memset(ones1[:], 1.0)
            g.memset(onesb[:], 1.0)
            inc(g.memset(onesk[:], 1.0), sONE, 1)
        ones_ms = sONE.n

        bias_ms = [0]

        def conv_slab(j):
            m, ot = j // CT, j % CT
            slot = j % 2

            @block.gpsimd
            def _(g):
                fs, fv = slot_free_wa[slot]
                W(g, "G", fs, fv)
                inc(g.dma_start(out=was[slot][:], in_=aw[m][ot]), sWB[slot], 16)

            if m < 3:
                @block.tensor
                def _(t):
                    W(t, "T", sWB[slot], sWB[slot].n)
                    W(t, "T", sDV, ck_ms)
                    for ct in range(CT):
                        mm = t.matmul(PS_D1[:, j:j + 1],
                                      was[slot][:, ct * 128:(ct + 1) * 128],
                                      ckv[:, m * CT + ct:m * CT + ct + 1],
                                      start=(ct == 0), stop=(ct == CT - 1))
                    inc(mm, sPE, 1)

            @block.vector
            def _(v):
                W(v, "V", sLV, 16)
                if m < 3:
                    W(v, "V", sPE, sPE.n)
                else:
                    W(v, "V", sWB[slot], sWB[slot].n)
                W(v, "V", sAWR, 16 * j)
                if m < 3:
                    for ct in range(CT):
                        v.tensor_scalar(wab[1][:, ct * 128:(ct + 1) * 128],
                                        was[slot][:, ct * 128:(ct + 1) * 128],
                                        vt[:, (V_TMK + m) * CT + ct:
                                           (V_TMK + m) * CT + ct + 1],
                                        None, ALU.mult)
                else:
                    v.tensor_copy(wab[1][:], was[slot][:])
                if j == 3 * CT - 1:
                    v.tensor_tensor(bias[:, 0:CT], PS_D1[:, 0:CT],
                                    vt[:, V_TF * CT:(V_TF + 1) * CT], ALU.add)
                    v.tensor_copy(bias[:, CT:3 * CT], PS_D1[:, CT:3 * CT])
                inc(v.drain(), sDV, 1)
                slot_free_wa[slot] = (sDV, sDV.n)
                if j == 3 * CT - 1:
                    bank_free[id(PS_D1)] = (sDV, sDV.n)
            if j == 3 * CT - 1:
                bias_ms[0] = sDV.n
            cms = sDV.n

            @block.sync
            def _(y):
                W(y, "Y", sDV, cms)
                inc(y.dma_start(out=acache[m, ot], in_=wab[1][:]), sAWR, 16)

        def conv_finish():
            @block.sync
            def _(y):
                W(y, "Y", sAWR, sAWR.n)



        mat_dve_done = [0] * max(cfg.SPC, 1)
        a1_hooks = []


        @block.gpsimd
        def _(g):
            for k0 in range(min(2, cfg.SPC)):
                inc(g.dma_start(out=mi[k0][:, 0:C], in_=gidx[k0]), sMI[k0], 16)

        for k in range(cfg.SPC):
            slot = k % 2

            @block.vector
            def _(v, k=k, slot=slot):
                W(v, "V", sMI[slot], 16 * (k // 2 + 1))
                if k >= 2:
                    W(v, "V", sSW, 16 * (k - 1))
                v.tensor_copy(matbf[:], mi[slot][:, 0:C])
                mo = mato[k % 2]
                v.tensor_scalar(mo[:], matbf[:], 0.0, float(lut_vals[0]),
                                ALU.is_equal, ALU.mult)
                for j in range(1, 16):
                    v.tensor_scalar(mattmp[:], matbf[:], float(j),
                                    float(lut_vals[j]), ALU.is_equal, ALU.mult)
                    lastv = v.tensor_tensor(mo[:], mo[:], mattmp[:], ALU.add)
                inc(lastv, sDV, 1)
                mat_dve_done[k] = sDV.n

            @block.gpsimd
            def _(g, k=k, slot=slot):
                W(g, "G", sDV, mat_dve_done[k])
                if k + 2 < cfg.SPC:
                    inc(g.dma_start(out=mi[slot][:, 0:C], in_=gidx[k + 2]),
                        sMI[slot], 16)
                if cfg.use_collective:
                    dst = cc_in[k * 128 * C:(k + 1) * 128 * C].rearrange(
                        "(p f) -> p f", p=128)
                else:
                    dst = cc_out[k * 128 * C:(k + 1) * 128 * C].rearrange(
                        "(p f) -> p f", p=128)
                inc(g.dma_start(out=dst, in_=mato[k % 2][:]), sSW, 16)
                if cfg.use_collective:
                    W(g, "G", sSW, 16 * (k + 1))
                    gsz = cfg.NCORES * 128 * C
                    inc(g.collective_compute(
                        "AllGather", ALU.bypass,
                        replica_groups=[list(range(cfg.NCORES))],
                        ins=[cc_in[k * 128 * C:(k + 1) * 128 * C]
                             .rearrange("(r c) -> r c", c=C).opt()],
                        outs=[cc_out[k * gsz:(k + 1) * gsz]
                              .rearrange("(r c) -> r c", c=C).opt()],
                    ), sCC, 1)
                else:
                    inc(g.nop(), sCC, 1)
            for fn in a1_hooks:
                fn(k)

        @block.gpsimd
        def _(g):
            W(g, "G", sSW, sSW.n)


        def make_stats_hook(ch, xset, same_phase, bf16_ok=True):
            st_state = {"pe": [0, 0], "ms": 0}

            def hook(ct, xr=None):
                b = ct % 2
                xrs, xrv = xr if (same_phase and xr is not None) \
                    else x_ready[ch][ct]

                @block.scalar
                def _(a, ct=ct, b=b, xrs=xrs, xrv=xrv):
                    W(a, "A", xrs, xrv)
                    if ct == 0:
                        W(a, "A", sLX, 16 * CT * (ch + 1))
                    if ct >= 2:
                        W(a, "A", sPE, st_state["pe"][b])
                    if bf16_ok:
                        a.activation(lnT[:, ct * CH:(ct + 1) * CH],
                                     xs[xset][:, ct * CH:(ct + 1) * CH],
                                     AF.Identity)
                        inc(a.activation(sqb[b][:, 0:CH],
                                         xs[xset][:, ct * CH:(ct + 1) * CH],
                                         AF.Square), sAC, 1)
                    else:
                        inc(a.activation(sq[b][:, 0:CH],
                                         xs[xset][:, ct * CH:(ct + 1) * CH],
                                         AF.Square), sAC, 1)

                @block.tensor
                def _(t, ct=ct, b=b, xrs=xrs, xrv=xrv):
                    W(t, "T", sAC, sAC.n)
                    W(t, "T", xrs, xrv)
                    if ct == 0:
                        W(t, "T", sLX, 16 * CT * (ch + 1))
                        W(t, "T", sONE, ones_ms)
                        for p in (PS_S1, PS_S2):
                            bs, bv = bank_free[id(p)]
                            W(t, "T", bs, bv)
                    if bf16_ok:
                        t.matmul(PS_S1[0:1, 0:CH], onesb[:],
                                 lnT[:, ct * CH:(ct + 1) * CH],
                                 start=(ct == 0), stop=(ct == CT - 1))
                        mm = t.matmul(PS_S2[0:1, 0:CH], onesb[:],
                                      sqb[b][:, 0:CH],
                                      start=(ct == 0), stop=(ct == CT - 1))
                    else:
                        t.matmul(PS_S1[0:1, 0:CH], onesk[:],
                                 xs[xset][:, ct * CH:(ct + 1) * CH],
                                 start=(ct == 0), stop=(ct == CT - 1))
                        mm = t.matmul(PS_S2[0:1, 0:CH], onesk[:], sq[b][:, 0:CH],
                                      start=(ct == 0), stop=(ct == CT - 1))
                    inc(mm, sPE, 1)
                    st_state["pe"][b] = sPE.n
                    if ct == CT - 1:
                        st_state["ms"] = sPE.n
            return hook, st_state

        def emit_ln(ch, xset, tag, stats_done=None):
            nonlocal ln_free
            sl = slice(0, CH)

            if stats_done is None:
                hook, st_state = make_stats_hook(ch, xset, same_phase=False,
                                                 bf16_ok=False)
                for ct in range(CT):
                    hook(ct)
                stats_ms = st_state["ms"]
            else:
                stats_ms = stats_done
            bank_ready[id(PS_S1)] = (sPE, stats_ms)
            bank_ready[id(PS_S2)] = (sPE, stats_ms)

            m_row = rows[0:1, 0:CH]
            v_row = rows[0:1, CH:2 * CH]
            a_row = rows[0:1, 2 * CH:3 * CH]
            b_row = rows[0:1, 3 * CH:4 * CH]
            t_row = rows[0:1, 4 * CH:5 * CH]
            t2_row = v_row

            @block.vector
            def _(v):
                W(v, "V", sPE, stats_ms)
                v.tensor_scalar(m_row, PS_S1[0:1, sl], 1.0 / C, None, ALU.mult)
                v.tensor_scalar(v_row, PS_S2[0:1, sl], 1.0 / C, 1e-5,
                                ALU.mult, ALU.add)
                v.drain()
                v.tensor_tensor(t_row, m_row, m_row, ALU.mult)
                v.drain()
                v.tensor_tensor(v_row, v_row, t_row, ALU.subtract)
                inc(v.drain(), sDV, 1)
                bank_free[id(PS_S1)] = (sDV, sDV.n)
                bank_free[id(PS_S2)] = (sDV, sDV.n)
                EMIT_LOG.append(("rows_read", ch, tag, stats_ms, sDV.n))

            @block.scalar
            def _(a):
                W(a, "A", sDV, sDV.n)
                a.activation(t_row, v_row, AF.Ln)
                inc(a.drain(), sAC, 1)

            @block.vector
            def _(v):
                W(v, "V", sAC, sAC.n)
                v.tensor_scalar(t2_row, t_row, -0.5, None, ALU.mult)
                inc(v.drain(), sDV, 1)

            @block.scalar
            def _(a):
                W(a, "A", sDV, sDV.n)
                a.activation(a_row, t2_row, AF.Exp)
                inc(a.drain(), sAC, 1)

            @block.vector
            def _(v):
                W(v, "V", sAC, sAC.n)
                v.tensor_tensor(t_row, m_row, a_row, ALU.mult)
                v.drain()
                v.tensor_scalar(b_row, t_row, -1.0, None, ALU.mult)
                inc(v.drain(), sDV, 1)
            rows_ms = sDV.n

            @block.tensor
            def _(t):
                W(t, "T", sDV, rows_ms)
                for p in (PS_BA, PS_BB):
                    bs, bv = bank_free[id(p)]
                    W(t, "T", bs, bv)
                t.matmul(PS_BA[:, sl], ones1[:], a_row, start=True, stop=True)
                mm = t.matmul(PS_BB[:, sl], ones1[:], b_row, start=True, stop=True)
                inc(mm, sPE, 1)
                bank_ready[id(PS_BA)] = (sPE, sPE.n)
                bank_ready[id(PS_BB)] = (sPE, sPE.n)
            bc_ms = sPE.n

            ln_tiles = [0] * CT

            @block.vector
            def _(v):
                W(v, "V", sPE, bc_ms)
                fs, fv = ln_free
                W(v, "V", fs, fv)
                for ct in range(CT):
                    xap = xs[xset][:, ct * CH:(ct + 1) * CH]
                    v.tensor_tensor(tmpA[:, sl], xap, PS_BA[:, sl], ALU.mult)
                    inc(v.tensor_tensor(lnT[:, ct * CH:(ct + 1) * CH],
                                        tmpA[:, sl], PS_BB[:, sl], ALU.add),
                        sDV, 1)
                    ln_tiles[ct] = sDV.n
                bank_free[id(PS_BA)] = (sDV, sDV.n)
                bank_free[id(PS_BB)] = (sDV, sDV.n)
            return ln_tiles

        fs_cnt = [0]

        def ffn_slab_load(src_slabs, slot):
            if cfg.use_collective:
                need = max(s // cfg.NCORES for s, _ in src_slabs) + 1
            else:
                need = max(s for s, _ in src_slabs) + 1

            @block.sync
            def _(y):
                W(y, "Y", sCC, need)
                fsm, fvv = slot_free_wf[slot]
                W(y, "Y", fsm, fvv)
                for s, coff in src_slabs:
                    w = min(C, I - coff)
                    inc(y.dma_start(out=wfs[slot][:, coff:coff + w],
                                    in_=slab_ap(s)[:, 0:w]), sWF[slot], 16)

        conv_queue = list(range(4 * CT))
        assert 3 * CT <= 2 * IT

        def emit_ffn(ch, xset, ln_ms, conv_hook=None, tail_hook=None):
            nonlocal sg_free
            sl = slice(0, CH)
            h_ms = 0
            silu_ms = {}
            for it in range(IT):
                if conv_hook is not None and conv_queue:
                    for _ in range(2):
                        if conv_queue:
                            conv_hook(conv_queue.pop(0))
                for half in (0, 1):
                    slot = fs_cnt[0] % 2
                    fs_cnt[0] += 1
                    ffn_slab_load([(2 * it + half, 0)], slot)

                    @block.tensor
                    def _(t, it=it, half=half, slot=slot):
                        W(t, "T", sWF[slot], sWF[slot].n)
                        p = PS_G if half == 0 else PS_U
                        bs, bv = bank_free[id(p)]
                        W(t, "T", bs, bv)
                        for ht in range(CT):
                            W(t, "T", sDV, ln_ms[ht])
                            mm = t.matmul(p[:, sl],
                                          wfs[slot][:, ht * 128:(ht + 1) * 128],
                                          lnT[:, ht * CH:(ht + 1) * CH],
                                          start=(ht == 0), stop=(ht == CT - 1))
                        inc(mm, sPE, 1)
                        bank_ready[id(p)] = (sPE, sPE.n)
                        slot_free_wf[slot] = (sPE, sPE.n)

                @block.scalar
                def _(a, it=it):
                    rs, rv = bank_ready[id(PS_G)]
                    W(a, "A", rs, rv)
                    fs2, fv2 = sg_free
                    W(a, "A", fs2, fv2)
                    inc(a.activation(sg[:, sl], PS_G[:, sl], AF.Silu), sAC, 1)
                    silu_ms[it] = sAC.n
                    bank_free[id(PS_G)] = (sAC, sAC.n)

                @block.vector
                def _(v, it=it):
                    W(v, "V", sAC, silu_ms[it])
                    rs, rv = bank_ready[id(PS_U)]
                    W(v, "V", rs, rv)
                    inc(v.tensor_tensor(hT[:, it * CH:(it + 1) * CH], sg[:, sl],
                                        PS_U[:, sl], ALU.mult), sDV, 1)
                    bank_free[id(PS_U)] = (sDV, sDV.n)
                    sg_free = (sDV, sDV.n)
                h_ms = sDV.n

            res_list = [None] * CT
            for ot in range(CT):
                if ot >= 2 and tail_hook is not None:
                    tail_hook(ot - 2, res_list[ot - 2])
                slot = fs_cnt[0] % 2
                fs_cnt[0] += 1
                ffn_slab_load([(2 * IT + ot * DS + j, j * C) for j in range(DS)],
                              slot)
                p = PS_D0 if ot % 2 == 0 else PS_D1

                @block.tensor
                def _(t, ot=ot, slot=slot, p=p):
                    W(t, "T", sWF[slot], sWF[slot].n)
                    W(t, "T", sDV, h_ms)
                    bs, bv = bank_free[id(p)]
                    W(t, "T", bs, bv)
                    for it in range(IT):
                        mm = t.matmul(p[:, sl],
                                      wfs[slot][:, it * 128:(it + 1) * 128],
                                      hT[:, it * CH:(it + 1) * CH],
                                      start=(it == 0), stop=(it == IT - 1))
                    inc(mm, sPE, 1)
                    bank_ready[id(p)] = (sPE, sPE.n)
                    slot_free_wf[slot] = (sPE, sPE.n)

                @block.vector
                def _(v, ot=ot, p=p):
                    rs, rv = bank_ready[id(p)]
                    W(v, "V", rs, rv)
                    inc(v.scalar_tensor_tensor(
                        xs[xset][:, ot * CH:(ot + 1) * CH], p[:, sl], 0.5,
                        xs[xset][:, ot * CH:(ot + 1) * CH], ALU.mult, ALU.add),
                        sDV, 1)
                    bank_free[id(p)] = (sDV, sDV.n)
                res_list[ot] = (sDV, sDV.n)
            if tail_hook is not None:
                for ct in range(CT - 2, CT):
                    tail_hook(ct, res_list[ct])
            return res_list

        a_cnt = [4 * CT]

        def attn_slab_load(m, ot, slot):
            @block.sync
            def _(y):
                fsm, fvv = slot_free_wa[slot]
                W(y, "Y", fsm, fvv)
                inc(y.dma_start(out=wab[slot][:], in_=acache[m, ot]), sWA[slot], 16)

        def emit_attn(ch, xset, ln_ms, tail_hook=None):
            nonlocal kvr_free, rwkv_cnt, ln_free
            sl = slice(0, CH)
            chain_ms = 0
            for ot in range(CT):
                psum_of = {0: PS_G, 1: PS_U, 2: PS_D0}
                for m in range(3):
                    slot = a_cnt[0] % 2
                    a_cnt[0] += 1
                    attn_slab_load(m, ot, slot)
                    p = psum_of[m]

                    @block.tensor
                    def _(t, m=m, ot=ot, slot=slot, p=p):
                        W(t, "T", sWA[slot], sWA[slot].n)
                        bs, bv = bank_free[id(p)]
                        W(t, "T", bs, bv)
                        for ct in range(CT):
                            W(t, "T", sDV, ln_ms[ct])
                            mm = t.matmul(p[:, sl],
                                          wab[slot][:, ct * 128:(ct + 1) * 128],
                                          lnT[:, ct * CH:(ct + 1) * CH],
                                          start=(ct == 0), stop=(ct == CT - 1))
                        inc(mm, sPE, 1)
                        bank_ready[id(p)] = (sPE, sPE.n)
                        slot_free_wa[slot] = (sPE, sPE.n)

                @block.scalar
                def _(a, ot=ot):
                    W(a, "A", sDV, bias_ms[0])
                    fs2, fv2 = kvr_free
                    W(a, "A", fs2, fv2)
                    for m, (p, dst) in enumerate(((PS_G, k_s), (PS_U, v_s),
                                                  (PS_D0, r_s))):
                        rs, rv = bank_ready[id(p)]
                        W(a, "A", rs, rv)
                        fn = AF.Sigmoid if m == 2 else AF.Identity
                        lasta = a.activation(
                            dst[:, sl], p[:, sl], fn,
                            bias=bias[:, m * CT + ot:m * CT + ot + 1],
                            scale=1.0)
                    inc(lasta, sAC, 1)
                    bank_free[id(PS_G)] = (sAC, sAC.n)
                    bank_free[id(PS_U)] = (sAC, sAC.n)
                    bank_free[id(PS_D0)] = (sAC, sAC.n)
                evac_ms = sAC.n

                @block.vector
                def _(v, ot=ot):
                    W(v, "V", sAC, evac_ms)
                    v.tensor_scalar(tmpA[:, sl], k_s[:, sl], 0.0, None, ALU.min)
                    lastv = v.tensor_scalar(tmpC[:, sl], k_s[:, sl], -1.0, 0.0,
                                            ALU.mult, ALU.max)
                    inc(lastv, sDV, 1)
                m2_ms = sDV.n

                @block.scalar
                def _(a, ot=ot):
                    W(a, "A", sDV, m2_ms)
                    a.activation(tmpB[:, sl], tmpA[:, sl], AF.Exp)
                    lasta = a.activation(tmpD[:, sl], tmpC[:, sl], AF.Exp)
                    inc(lasta, sAC, 1)
                exp_ms = sAC.n
                if debug and ch == 0 and ot == 0:
                    @block.gpsimd
                    def _(g):
                        W(g, "G", sAC, evac_ms)
                        inc(g.dma_start(out=dbg["d_kvr"][:, 0:CH],
                                        in_=k_s[:, sl]), sSX, 16)
                        inc(g.dma_start(out=dbg["d_kvr"][:, CH:2 * CH],
                                        in_=v_s[:, sl]), sSX, 16)
                        inc(g.dma_start(out=dbg["d_kvr"][:, 2 * CH:3 * CH],
                                        in_=r_s[:, sl]), sSX, 16)

                @block.vector
                def _(v, ot=ot):
                    W(v, "V", sAC, exp_ms)
                    v.tensor_tensor(tmpB[:, sl], tmpB[:, sl], v_s[:, sl], ALU.mult)
                    v.tensor_tensor(tmpB[:, sl], tmpB[:, sl], tmpD[:, sl], ALU.mult)
                    inc(v.tensor_tensor(rwkv[:, ot * CH:(ot + 1) * CH],
                                        tmpB[:, sl], r_s[:, sl], ALU.mult),
                        sDV, 1)

                if ch == 0:
                    f1 = fx[:, 0:1]; f2 = fx[:, 1:2]; f3 = fx[:, 2:3]
                    f4 = fx[:, 3:4]; f5 = fx[:, 4:5]; f6 = fx[:, 5:6]
                    f7 = fx[:, 6:7]; f8 = fx[:, 7:8]
                    ppc = vt[:, V_PP * CT + ot:V_PP * CT + ot + 1]
                    aac = vt[:, V_AA * CT + ot:V_AA * CT + ot + 1]
                    bbc = vt[:, V_BB * CT + ot:V_BB * CT + ot + 1]

                    @block.vector
                    def _(v, ot=ot):
                        v.tensor_tensor(f1, ppc, k_s[:, 0:1], ALU.max)
                        v.drain()
                        v.tensor_tensor(f2, ppc, f1, ALU.subtract)
                        v.tensor_tensor(f3, k_s[:, 0:1], f1, ALU.subtract)
                        inc(v.drain(), sDV, 1)

                    @block.scalar
                    def _(a, ot=ot):
                        W(a, "A", sDV, sDV.n)
                        a.activation(f6, f2, AF.Exp)
                        a.activation(f7, f3, AF.Exp)
                        inc(a.drain(), sAC, 1)

                    @block.vector
                    def _(v, ot=ot):
                        W(v, "V", sAC, sAC.n)
                        v.tensor_tensor(f4, f6, aac, ALU.mult)
                        v.tensor_tensor(f5, f7, v_s[:, 0:1], ALU.mult)
                        v.drain()
                        v.tensor_tensor(f4, f4, f5, ALU.add)
                        v.tensor_tensor(f2, f6, bbc, ALU.mult)
                        v.drain()
                        v.tensor_tensor(f2, f2, f7, ALU.add)
                        v.drain()
                        v.reciprocal(f8, f2)
                        v.drain()
                        v.tensor_tensor(f4, f4, f8, ALU.mult)
                        v.drain()
                        v.tensor_tensor(rwkv[:, ot * CH:ot * CH + 1], f4,
                                        r_s[:, 0:1], ALU.mult)
                        inc(v.drain(), sDV, 1)

                chain_ms = sDV.n
                kvr_free = (sDV, chain_ms)

            res_list = [None] * CT
            for ot in range(CT):
                if ot >= 2 and tail_hook is not None:
                    tail_hook(ot - 2, res_list[ot - 2])
                slot = a_cnt[0] % 2
                a_cnt[0] += 1
                attn_slab_load(3, ot, slot)
                p = PS_D1 if ot % 2 == 0 else PS_D0

                @block.tensor
                def _(t, ot=ot, slot=slot, p=p):
                    W(t, "T", sWA[slot], sWA[slot].n)
                    W(t, "T", sDV, chain_ms)
                    bs, bv = bank_free[id(p)]
                    W(t, "T", bs, bv)
                    for ct in range(CT):
                        mm = t.matmul(p[:, sl],
                                      wab[slot][:, ct * 128:(ct + 1) * 128],
                                      rwkv[:, ct * CH:(ct + 1) * CH],
                                      start=(ct == 0), stop=(ct == CT - 1))
                    inc(mm, sPE, 1)
                    bank_ready[id(p)] = (sPE, sPE.n)
                    slot_free_wa[slot] = (sPE, sPE.n)
                if ot == CT - 1:
                    ln_free = (sPE, sPE.n)

                @block.vector
                def _(v, ot=ot, p=p):
                    rs, rv = bank_ready[id(p)]
                    W(v, "V", rs, rv)
                    inc(v.scalar_tensor_tensor(
                        xs[xset][:, ot * CH:(ot + 1) * CH], p[:, sl], 1.0,
                        xs[xset][:, ot * CH:(ot + 1) * CH], ALU.mult, ALU.add),
                        sDV, 1)
                    bank_free[id(p)] = (sDV, sDV.n)
                res_list[ot] = (sDV, sDV.n)
            if tail_hook is not None:
                for ct in range(CT - 2, CT):
                    tail_hook(ct, res_list[ct])
            return res_list

        def dump(name, src_ap, width, wait_sem, wait_val):
            if not debug:
                return
            @block.gpsimd
            def _(g):
                W(g, "G", wait_sem, wait_val)
                inc(g.dma_start(out=dbg[name], in_=src_ap), sSX, 16)

        def dump_dram(name, src_ap, width, wait_sem, wait_val):
            if not debug:
                return
            @block.gpsimd
            def _(g):
                W(g, "G", wait_sem, wait_val)
                inc(g.dma_start(out=dbg[name], in_=src_ap), sSX, 16)

        if debug:
            dump_dram("d_slab", slab_ap(0), C, sCC, 1)
            dump("d_ckv", ckv[:], 3 * CT, sDV, ck_ms)
            dump("d_vt", vt[:], 8 * CT, sLV, 16)

        store_ms = [0] * (NCH + 2)

        def emit_xload(ch):
            if ch == 0 or ch >= NCH:
                return

            @block.gpsimd
            def _(g, ch=ch):
                if ch >= 2:
                    W(g, "G", sSX, store_ms[ch - 2])
                for ct in range(CT):
                    inc(g.dma_start(
                        out=xs[ch % 2][:, ct * CH:(ct + 1) * CH],
                        in_=xT[ct, :, ch * CH:(ch + 1) * CH]), sLX, 16)

        def emit_ffn1(ch, stats_done=None, tail_hook=None):
            xset = ch % 2
            ln1 = emit_ln(ch, xset, "ffn1", stats_done=stats_done)
            if ch == 0:
                dump("d_ln1", lnT[:], CT * CH, sDV, ln1[-1])
            r1 = emit_ffn(ch, xset, ln1,
                          conv_hook=(conv_slab if ch <= 1 else None),
                          tail_hook=tail_hook)
            x_ready[ch] = r1
            if ch == 0:
                dump("d_h", hT[:], IT * CH, sDV, r1[-1][1])
                dump("d_x1", xs[xset][:], CT * CH, sDV, r1[-1][1])

        def emit_rest(ch, pre_attn_stats=None, next_hook=None, use_hook2=True):
            xset = ch % 2
            lna = emit_ln(ch, xset, "attn", stats_done=pre_attn_stats)
            if use_hook2:
                hook2, st2 = make_stats_hook(ch, xset, same_phase=True)
            r2 = emit_attn(ch, xset, lna,
                           tail_hook=(hook2 if use_hook2 else None))
            x_ready[ch] = r2
            if ch == 0:
                dump("d_x2", xs[xset][:], CT * CH, sDV, r2[-1][1])
            ln2 = emit_ln(ch, xset, "ffn2",
                          stats_done=(st2["ms"] if use_hook2 else None))
            if ch == 0:
                dump("d_ln3", lnT[:], CT * CH, sDV, ln2[-1])
            r3 = emit_ffn(ch, xset, ln2, tail_hook=next_hook)
            x_ready[ch] = r3

            @block.gpsimd
            def _(g, ch=ch, xset=xset, r3=r3):
                W(g, "G", sDV, r3[-1][1])
                for ct in range(CT):
                    inc(g.dma_start(out=outT[ct, :, ch * CH:(ch + 1) * CH],
                                    in_=xs[xset][:, ct * CH:(ct + 1) * CH]),
                        sSX, 16)
            store_ms[ch] = sSX.n

        emit_xload(0)
        emit_xload(1)
        HOOKS = True
        HOOK2 = True
        emit_ffn1(0)
        hA0, stA0 = make_stats_hook(0, 0, same_phase=False)
        emit_ffn1(1, tail_hook=(hA0 if HOOKS else None))
        conv_finish()
        if debug:
            dump_dram("d_aw", acache[0, 0], C, sAWR, 16 * 1)
            dump("d_bias", bias[:], 3 * CT, sDV, bias_ms[0])
        hooks = {}
        if NCH > 1:
            hooks[1], hooks["s1"] = make_stats_hook(1, 1, same_phase=False)
        if NCH > 2:
            hooks[2], hooks["s2"] = make_stats_hook(2, 0, same_phase=False)
        if NCH > 3:
            hooks[3], hooks["s3"] = make_stats_hook(3, 1, same_phase=False)
        emit_rest(0, use_hook2=HOOK2, pre_attn_stats=(stA0["ms"] if HOOKS else None),
                  next_hook=(hooks.get(1) if HOOKS else None))
        emit_xload(2)
        emit_rest(1, use_hook2=HOOK2,
                  pre_attn_stats=(hooks["s1"]["ms"] if HOOKS and NCH > 1 else None),
                  next_hook=(hooks.get(2) if HOOKS else None))
        emit_xload(3)
        if NCH > 2:
            emit_ffn1(2, stats_done=(hooks["s2"]["ms"] if HOOKS else None))
            emit_rest(2, use_hook2=HOOK2, pre_attn_stats=None,
                      next_hook=(hooks.get(3) if HOOKS else None))
        if NCH > 3:
            emit_ffn1(3, stats_done=(hooks["s3"]["ms"] if HOOKS else None))
            emit_rest(3, use_hook2=HOOK2, pre_attn_stats=None, next_hook=None)

        @block.gpsimd
        def _(g):
            W(g, "G", sSX, store_ms[NCH - 1])

    return nc



def _wstream(w, CT):
    wT = np.ascontiguousarray(np.asarray(w, dtype=np.float32).T)
    C = wT.shape[0]
    s = wT.reshape(CT, 128, CT, 128).transpose(2, 1, 0, 3)
    return np.ascontiguousarray(s).reshape(CT, 128, C)


def _ghost_stream(gate_idx, up_idx, down_idx, cfg):
    C, I, CT, IT, DS = cfg.C, cfg.I, cfg.CT, cfg.IT, cfg.DS
    gT = np.ascontiguousarray(np.asarray(gate_idx, dtype=np.int32).T)
    uT = np.ascontiguousarray(np.asarray(up_idx, dtype=np.int32).T)
    dT = np.ascontiguousarray(np.asarray(down_idx, dtype=np.int32).T)
    g = gT.reshape(CT, 128, IT, 128).transpose(2, 1, 0, 3).reshape(IT, 128, C)
    u = uT.reshape(CT, 128, IT, 128).transpose(2, 1, 0, 3).reshape(IT, 128, C)
    g = np.stack([g, u], axis=1).reshape(2 * IT, 128, C)
    d = dT.reshape(IT, 128, CT, 128).transpose(2, 1, 0, 3).reshape(CT, 128, I)
    dpad = np.zeros((CT, 128, cfg.IPAD), dtype=np.int32)
    dpad[:, :, :I] = d
    d_sl = dpad.reshape(CT, 128, DS, C).transpose(0, 2, 1, 3)
    flat = np.concatenate([g.reshape(-1), d_sl.reshape(-1)])
    pad = np.zeros(cfg.S_PAD * 128 * C, dtype=np.int32)
    pad[:flat.size] = flat
    return pad.reshape(cfg.S_PAD, 128, C)


def _core_slabs(ghost, cfg, r):
    if not cfg.use_collective:
        return ghost
    ks = [cfg.NCORES * k + r for k in range(cfg.SPC)]
    return np.ascontiguousarray(ghost[ks])


def _vecs_for_core(tf, tmk, tmv, tmr, xx, aa, bb, pp, cfg, seq_start):
    C, CT = cfg.C, cfg.CT
    z = np.zeros(C, dtype=np.float32)
    neg = np.full(C, -1e38, dtype=np.float32)
    rows = [np.asarray(v, dtype=np.float32).reshape(C) for v in
            (tf, tmk, tmv, tmr, xx)]
    rows += [np.asarray(aa, np.float32).reshape(C) if seq_start else z,
             np.asarray(bb, np.float32).reshape(C) if seq_start else z,
             np.asarray(pp, np.float32).reshape(C) if seq_start else neg]
    v2 = np.stack(rows)
    return np.ascontiguousarray(
        v2.reshape(8, CT, 128).transpose(2, 0, 1).reshape(128, 8 * CT))


def run(cfg, inputs):
    x = np.asarray(inputs["L0_attn_x"], dtype=np.float32)
    B, T, C = x.shape
    xf = x.reshape(B * T, C)
    lut = [float(v) for v in np.asarray(inputs["L0_ffn_lut"], dtype=np.float32)]

    ghost = _ghost_stream(inputs["L0_ffn_gate_idx"], inputs["L0_ffn_up_idx"],
                          inputs["L0_ffn_down_idx"], cfg)
    aw_streams = [_wstream(inputs[k], cfg.CT) for k in
                  ("L0_attn_w_key", "L0_attn_w_value", "L0_attn_w_rec",
                   "L0_attn_w_out")]

    in_maps = []
    for r in range(cfg.NCORES):
        xc = xf[r * cfg.TCORE:(r + 1) * cfg.TCORE]
        xTc = np.ascontiguousarray(xc.T).reshape(cfg.CT, 128, cfg.TCORE)
        seq_start = (r * cfg.TCORE) % cfg.SEQ == 0
        im = {
            "xT": xTc,
            "vecs": _vecs_for_core(
                inputs["L0_attn_time_first"], inputs["L0_attn_time_mix_k"],
                inputs["L0_attn_time_mix_v"], inputs["L0_attn_time_mix_r"],
                inputs["L0_attn_xx"], inputs["L0_attn_aa"],
                inputs["L0_attn_bb"], inputs["L0_attn_pp"], cfg, seq_start),
            "gidx": _core_slabs(ghost, cfg, r),
        }
        for m in range(4):
            im[f"aw{m}"] = aw_streams[m]
        in_maps.append(im)

    nc = build_program(cfg, lut)
    res = run_bass_kernel_spmd(nc, in_maps, core_ids=list(range(cfg.NCORES)))

    outs = []
    for r in range(cfg.NCORES):
        o = np.asarray(res.results[r]["outT"], dtype=np.float32)
        outs.append(np.ascontiguousarray(o.reshape(C, cfg.TCORE).T))
    return np.concatenate(outs, axis=0).reshape(B, T, C)


def kernel(**inputs):
    return run(CFG_FULL, inputs)
